# version 07
# baseline (speedup 1.0000x reference)
import math
import contextlib
import numpy as np
import ml_dtypes
import concourse.bass as bass
import concourse.mybir as mybir
from concourse.bass_utils import run_bass_kernel_spmd

F32 = mybir.dt.float32
BF16 = mybir.dt.bfloat16
U8 = mybir.dt.uint8
AF = mybir.ActivationFunctionType
OP = mybir.AluOpType
AX = mybir.AxisListType
NBF = ml_dtypes.bfloat16

B, S, D = 4, 2048, 4096
H, HKV, HD = 32, 8, 128
REP = H // HKV
NSLOT = 8
TGT = 205
NPROBE = 13
G = 4
NG = H // G
MASKVAL = -28672.0
ISQ = float(1.0 / np.float32(math.sqrt(HD)))
C1 = 6.28125
C2 = float(np.float32(2 * math.pi - C1))
C3 = float(np.float32(2 * math.pi - C1 - np.float32(2 * math.pi - C1)))
SIGCLAMP = 80.0

KT = [2 * j + 2 for j in range(NSLOT)]
LS = [128 * k for k in KT]
NQT = 8


class Builder:
    def __init__(self):
        self.nc = bass.Bass("TRN2", target_bir_lowering=False)
        self.tasks = []
        self.ctr = {}
        self.sem = {}
        self.es = contextlib.ExitStack()

    def new_sem(self, *names):
        for n in names:
            self.sem[n] = self.es.enter_context(self.nc.semaphore(n))
            self.ctr[n] = 0

    def sb(self, name, shape, dt):
        return self.es.enter_context(self.nc.sbuf_tensor(name, shape, dt))

    def ps(self, name, shape, dt=F32):
        return self.es.enter_context(self.nc.psum_tensor(name, shape, dt))

    def t(self, eng, fn, waits=(), inc=None, amt=1):
        tok = None
        waits = [w for w in waits if w is not None]
        if inc is not None:
            if inc in ('ld', 'gd', 'st', 'lq', 'la', 'lx'):
                waits.append((inc, self.ctr[inc]))
            self.ctr[inc] += amt
            tok = self.ctr[inc]
        self.tasks.append((eng, fn, waits, inc, amt, tok))
        return tok

    def dma(self, q, out, in_, waits=(), inc='ld'):
        return self.t(q, lambda e: e.dma_start(out=out, in_=in_), waits=waits,
                      inc=inc, amt=16)

    def now(self, *engs):
        return [(e, self.ctr[e]) for e in engs if self.ctr[e] > 0]

    def emit(self, block):
        def run(engname, eng):
            hiwater = {}
            own = {}
            for teng, fn, waits, inc, amt, tok in self.tasks:
                if teng != engname:
                    continue
                for (s, v) in waits:
                    if v <= 0 or hiwater.get(s, 0) >= v:
                        continue
                    eng.wait_ge(self.sem[s], v)
                    hiwater[s] = v
                instr = fn(eng)
                if inc is not None:
                    assert instr is not None, f"{engname} task returned None"
                    instr.then_inc(self.sem[inc], amt)
                    own[inc] = max(own.get(inc, 0), tok)

        @block.sync
        def _(e):
            run('sy', e)

        @block.vector
        def _(e):
            run('ve', e)

        @block.scalar
        def _(e):
            run('ac', e)

        @block.tensor
        def _(e):
            run('pe', e)

        @block.gpsimd
        def _(e):
            run('po', e)


def cdiv(a, b):
    return (a + b - 1) // b


def chunked_wT(ap):
    return ap.rearrange("(c p) n -> p c n", p=128)


def build_program():
    b = Builder()
    nc = b.nc
    E = b.sb

    def din(name, shape, dt=F32):
        return nc.dram_tensor(name, shape, dt, kind="ExternalInput").ap()

    x_full = din("x_full", [S, D])
    x_own = din("x_own", [1024, D])
    wqT = din("wqT", [D, D])
    wkT = din("wkT", [D, HKV * HD])
    wvT = din("wvT", [D, HKV * HD])
    woT = din("woT", [D, D])
    wT = din("wT", [D, D])
    a_st = din("a_st", [D, 256])
    bf_st = din("bf_st", [256, D])
    fbias = din("fbias", [1, D])
    coeffs = din("coeffs", [16, 1])
    scales = din("scales", [16, 1])
    t_full = din("t_full", [1, S])
    t_own = din("t_own", [1, 1024])
    n_full = din("n_full", [64, S])
    n_own = din("n_own", [64, 1024])
    ivec = din("ivec", [64, 1])
    selc = din("selc", [128, NSLOT * 8])
    sbmask = din("sbmask", [128, NSLOT * 256])
    ptmask = din("ptmask", [128, 16 * 256])
    iota8 = din("iota8", [128, 8])

    import os as _os
    _dbg = "ExternalOutput" if _os.environ.get("KDBG") else "Internal"
    dbg = nc.dram_tensor("dbg", [4, 128, 1024], F32, kind=_dbg).ap()
    out_own = nc.dram_tensor("out_own", [1024, D], F32, kind="ExternalOutput").ap()
    attn_hbm = nc.dram_tensor("attn_hbm", [H, HD, 1024], BF16, kind=_dbg).ap()
    qr_hbm = nc.dram_tensor("qr_hbm", [H, 2, HD, 512], BF16, kind=_dbg).ap()
    xno_hbm = nc.dram_tensor("xno_hbm", [128, 32 * 1024], BF16, kind=_dbg).ap()
    zdram = nc.dram_tensor("zdram", [1, 256], F32, kind="Internal").ap()
    wdram = nc.dram_tensor("wdram", [16, 1], F32, kind="Internal").ap()

    b.new_sem('ld', 'gd', 'st', 've', 'ac', 'pe', 'po', 'lq', 'la', 'lx')

    ones1r = b.sb("ones1r", [1, 128], BF16)
    idbf = b.sb("idbf", [128, 128], BF16)

    pA = contextlib.ExitStack()
    E = lambda n, sh, dt: pA.enter_context(nc.sbuf_tensor(n, sh, dt))
    krT = E("krT", [128, HKV * S], BF16)
    vkm = E("vkm", [128, HKV * 16 * HD], BF16)
    cosq = E("cosq", [128, 1024], BF16)
    sinq = E("sinq", [128, 1024], BF16)
    cosk = E("cosk", [128, S], BF16)
    sink = E("sink", [128, S], BF16)
    selcs = E("selcs", [128, NSLOT * 8], F32)
    sbmasks = E("sbmasks", [128, NSLOT * 256], BF16)
    ptmasks = E("ptmasks", [128, 16 * 256], BF16)
    iota8s = E("iota8s", [128, 8], F32)
    ones128 = E("ones128", [128, 1], BF16)
    ones4r = E("ones4r", [4, 128], BF16)
    ones1f = E("ones1f", [1, 128], F32)
    ones64f = E("ones64f", [1, 64], F32)
    taustate = E("taustate", [128, H * NSLOT], F32)

    ST = {n: E("st_" + n, [128, 2 * G], F32)
          for n in ('LO', 'HI', 'NLO', 'NHI', 'T', 'CNT', 'F', 'SC1', 'SC2',
                    'TAU', 'JC')}
    STU = {n: E("stu_" + n, [128, 2 * G], U8) for n in ('GE', 'NE', 'UA', 'UB')}
    w8 = E("w8", [128, 2 * G * 8], F32)
    cpk = E("cpk", [128, 2 * G * 4], BF16)
    rrows = E("rrows", [4, NSLOT * G * 128], BF16)
    zrow = E("zrow", [1, 256], F32)
    zinv = E("zinv", [1, 256], F32)

    SL = {}
    for j in range(NSLOT):
        base = j * 8
        SL[j] = {n: selcs[:, base + i: base + i + 1] for i, n in enumerate(
            ('TGT', 'AIM', 'T32', 'T64A', 'T64B', 'KEEP', 'LP', 'NQ'))}

    a0 = b.dma('sy', selcs[:], selc)
    a1 = b.dma('sy', iota8s[:], iota8)
    a2 = b.t('po', lambda e: e.dma_start(out=sbmasks[:], in_=sbmask),
             inc='gd', amt=16)
    a3 = b.t('po', lambda e: e.dma_start(out=ptmasks[:], in_=ptmask),
             inc='gd', amt=16)

    def consts(e):
        e.memset(ones128[:], 1.0)
        e.memset(ones4r[:], 1.0)
        e.memset(ones1r[:], 1.0)
        e.memset(ones1f[:], 1.0)
        e.memset(ones64f[:], 1.0)
        return e.engine_nop()
    c0 = b.t('ve', consts, inc='ve')

    def mkident(e):
        e.memset(idbf[:], 0.0)
        return e.affine_select(out=idbf[:], in_=idbf[:], compare_op=OP.not_equal,
                               fill=1.0, base=0, pattern=[[-1, 128]],
                               channel_multiplier=1)
    cid = b.t('po', mkident, inc='po')


    p1 = contextlib.ExitStack()
    E1 = lambda n, s, d: p1.enter_context(nc.sbuf_tensor(n, s, d))
    ivs = E1("ivs", [64, 1], F32)
    k1e4 = E1("k1e4", [1, 1], F32)
    lnv = E1("lnv", [1, 1], F32)
    s64 = E1("s64", [64, 1], F32)
    invf = E1("invf", [64, 1], F32)
    trow = E1("trow", [1, S], F32)
    nrow = E1("nrow", [64, S], F32)
    theta = E1("theta", [64, S], F32)
    rrd = E1("rrd", [64, S], F32)
    ps_th = p1.enter_context(nc.psum_tensor("ps_th", [64, 512], F32))

    t1 = b.dma('sy', ivs[:], ivec)
    t1b = b.t('ve', lambda e: e.memset(s64[:], 10000.0), waits=[('ld', t1)],
              inc='ve')
    t2 = b.t('ac', lambda e: e.activation(out=s64[:], in_=s64[:], func=AF.Ln),
             waits=[('ve', t1b)], inc='ac')
    t4 = b.t('ve', lambda e: e.tensor_scalar(out=s64[:], in0=s64[:],
                                             scalar1=-1.0 / 64, scalar2=None,
                                             op0=OP.mult),
             waits=[('ac', t2)], inc='ve')
    t4b = b.t('ve', lambda e: e.tensor_tensor(out=s64[:], in0=s64[:], in1=ivs[:],
                                              op=OP.mult), waits=[('ve', t4)],
              inc='ve')
    t5 = b.t('ac', lambda e: e.activation(out=invf[:], in_=s64[:], func=AF.Exp),
             waits=[('ve', t4b)], inc='ac')

    def build_tables(tdram, ndram, n_t, cosd, sind):
        nl = b.dma('sy', nrow[:, :n_t], ndram, waits=b.now('ve'))
        trep = b.t('sy', lambda e: e.dma_start(
            out=theta[:, :n_t],
            in_=tdram.rearrange("p (o n) -> p o n", o=1).to_broadcast([1, 64, n_t])),
            waits=b.now('pe', 've', 'ac'), inc='ld', amt=16)
        last = b.t('ve', lambda e: e.tensor_tensor(
            out=theta[:, :n_t], in0=theta[:, :n_t],
            in1=invf[:, 0:1].to_broadcast([64, n_t]), op=OP.mult),
            waits=[('ld', trep), ('ac', t5)], inc='ve')

        def cw(e):
            e.scalar_tensor_tensor(out=rrd[:, :n_t], in0=nrow[:, :n_t],
                                   scalar=-C1, in1=theta[:, :n_t],
                                   op0=OP.mult, op1=OP.add)
            e.scalar_tensor_tensor(out=rrd[:, :n_t], in0=nrow[:, :n_t],
                                   scalar=-C2, in1=rrd[:, :n_t],
                                   op0=OP.mult, op1=OP.add)
            return e.scalar_tensor_tensor(out=rrd[:, :n_t], in0=nrow[:, :n_t],
                                          scalar=-C3, in1=rrd[:, :n_t],
                                          op0=OP.mult, op1=OP.add)
        tcw = b.t('ve', cw, waits=[('ld', nl), ('ve', last)], inc='ve')
        ts1 = b.t('ac', lambda e: e.activation(out=sind[0:64, :n_t],
                                               in_=rrd[:, :n_t], func=AF.Sin),
                  waits=[('ve', tcw)], inc='ac')
        ts2 = b.t('ac', lambda e: e.activation(out=theta[:, :n_t],
                                               in_=rrd[:, :n_t], func=AF.Sin,
                                               scale=0.5),
                  inc='ac')
        tc1 = b.t('ve', lambda e: e.scalar_tensor_tensor(
            out=theta[:, :n_t], in0=theta[:, :n_t], scalar=-2.0,
            in1=theta[:, :n_t], op0=OP.mult, op1=OP.mult),
            waits=[('ac', ts2)], inc='ve')
        tc2 = b.t('ve', lambda e: e.tensor_scalar(
            out=cosd[0:64, :n_t], in0=theta[:, :n_t], scalar1=1.0, scalar2=None,
            op0=OP.add), inc='ve')
        d1 = b.t('ve', lambda e: e.tensor_scalar(
            out=cosd[64:128, :n_t], in0=cosd[0:64, :n_t], scalar1=1.0,
            scalar2=None, op0=OP.mult), inc='ve')
        d2 = b.t('ve', lambda e: e.tensor_scalar(
            out=sind[64:128, :n_t], in0=sind[0:64, :n_t], scalar1=1.0,
            scalar2=None, op0=OP.mult), waits=[('ac', ts1)], inc='ve')
        return [('ac', ts1), ('ve', d1), ('ve', d2)]

    rk = build_tables(t_full, n_full, S, cosk, sink)
    rq = build_tables(t_own, n_own, 1024, cosq, sinq)

    p1.close()
    p2 = contextlib.ExitStack()
    E2 = lambda n, s, d: p2.enter_context(nc.sbuf_tensor(n, s, d))
    xtile = [E2("xtile0", [128, D], F32)]
    xnb = [E2("xnb0", [128, D], BF16)]
    lnmu = E2("lnmu", [128, 1], F32)
    lnnv = E2("lnnv", [128, 1], F32)
    lnrr = E2("lnrr", [128, 1], F32)
    rsc = E2("rsc", [128, 1024], BF16)
    xnT_blk = E2("xnT_blk", [128, 32 * 512], BF16)
    wbuf = [E2(f"wbuf{i}", [128, 2 * 32 * 128], BF16) for i in range(2)]
    kvtmp = E2("kvtmp", [128, 512], F32)
    kvb = E2("kvb", [128, 512], BF16)
    ps_tr = [p2.enter_context(nc.psum_tensor(f"ps_tr{i}", [128, 128], BF16))
             for i in range(2)]
    ps_mm = [p2.enter_context(nc.psum_tensor(f"ps_mm{i}", [128, 512], F32))
             for i in range(2)]

    def ln_tile(xt, dst, waits, mu, nv, rr):
        t1 = b.t('ve', lambda e: e.tensor_scalar(
            out=dst[:], in0=xt[:], scalar1=1.0, scalar2=None, op0=OP.mult,
            op1=OP.add, accum_out=mu[:]), waits=waits, inc='ve')
        t2 = b.t('ve', lambda e: e.tensor_scalar(
            out=mu[:], in0=mu[:], scalar1=-1.0 / D, scalar2=None,
            op0=OP.mult), waits=[('ve', t1)], inc='ve')
        t3 = b.t('ac', lambda e: e.activation(
            out=dst[:], in_=xt[:], func=AF.Square, bias=mu[:],
            accum_out=nv[:]), waits=[('ve', t2)], inc='ac')
        t4 = b.t('ve', lambda e: e.tensor_scalar(
            out=nv[:], in0=nv[:], scalar1=1.0 / D, scalar2=1e-5,
            op0=OP.mult, op1=OP.add), waits=[('ac', t3)], inc='ve')
        t5 = b.t('ac', lambda e: e.activation(out=nv[:], in_=nv[:],
                                              func=AF.Sqrt),
                 waits=[('ve', t4)], inc='ac')
        t6 = b.t('ve', lambda e: e.reciprocal(out=rr[:], in_=nv[:]),
                 waits=[('ac', t5)], inc='ve')
        return b.t('ve', lambda e: e.tensor_scalar(
            out=dst[:], in0=xt[:], scalar1=mu[:], scalar2=rr[:],
            op0=OP.add, op1=OP.mult), waits=[('ve', t6)], inc='ve')

    def transpose_to(src_bf, dst_ap_fn, n_chunks, extra_waits, trp):
        toks = []
        for c in range(n_chunks):
            pb = c % 2
            tk = b.t('pe', lambda e, c=c, pb=pb: e.transpose(
                out=trp[pb][:], in_=src_bf[:, c * 128:(c + 1) * 128],
                identity=idbf[:]),
                waits=extra_waits + [('po', cid)] + b.now('ve'), inc='pe')
            toks.append(b.t('ve', lambda e, c=c, pb=pb: e.tensor_copy(
                out=dst_ap_fn(c), in_=trp[pb][:]), waits=[('pe', tk)],
                inc='ve'))
        return toks

    for blk in range(4):
        for tt in range(4):
            buf = 0
            tok = blk * 512 + tt * 128
            tl = b.dma('sy', xtile[buf][:], x_full[tok:tok + 128, :],
                       waits=b.now('ve', 'po') + (rk + rq if (blk == 0 and tt == 0) else []))
            tln = ln_tile(xtile[buf], xnb[buf], [('ld', tl)], lnmu, lnnv, lnrr)
            transpose_to(
                xnb[buf],
                lambda c, tt=tt: xnT_blk[:, c * 512 + tt * 128:
                                         c * 512 + tt * 128 + 128],
                32, [('ve', tln)], ps_tr)
        xnT_tok = b.ctr['ve']
        for g in range(HKV):
            wb = g % 2
            wkl = b.t('po', lambda e, g=g, wb=wb: e.dma_start(
                out=wbuf[wb][:, 0:4096].rearrange("p (c n) -> p c n", n=128),
                in_=chunked_wT(wkT)[:, :, g * 128:(g + 1) * 128]),
                waits=b.now('pe'), inc='gd', amt=16)
            wvl = b.t('po', lambda e, g=g, wb=wb: e.dma_start(
                out=wbuf[wb][:, 4096:8192].rearrange("p (c n) -> p c n", n=128),
                in_=chunked_wT(wvT)[:, :, g * 128:(g + 1) * 128]),
                inc='gd', amt=16)
            for c in range(32):
                tk = b.t('pe', lambda e, c=c, wb=wb: e.matmul(
                    out=ps_mm[0][:], lhsT=wbuf[wb][:, c * 128:(c + 1) * 128],
                    rhs=xnT_blk[:, c * 512:(c + 1) * 512],
                    start=(c == 0), stop=(c == 31)),
                    waits=([('gd', wkl), ('ve', xnT_tok)] + b.now('ac'))
                    if c == 0 else [], inc='pe' if c == 31 else None)
            kev = b.t('ac', lambda e: e.copy(out=kvtmp[:], in_=ps_mm[0][:]),
                      waits=[('pe', tk)] + b.now('ve', 'po'), inc='ac')
            sl = slice(blk * 512, blk * 512 + 512)
            dst = krT[:, g * S + blk * 512: g * S + blk * 512 + 512]

            def ropek(e, dst=dst, sl=sl):
                e.tensor_copy(out=kvb[:], in_=kvtmp[:])
                e.tensor_scalar(out=rsc[0:64, 0:512], in0=kvb[64:128, :],
                                scalar1=-1.0, scalar2=None, op0=OP.mult)
                e.tensor_scalar(out=rsc[64:128, 0:512], in0=kvb[0:64, :],
                                scalar1=1.0, scalar2=None, op0=OP.mult)
                e.tensor_tensor(out=dst[:, :], in0=kvb[:], in1=cosk[:, sl],
                                op=OP.mult)
                e.tensor_tensor(out=rsc[:, 0:512], in0=rsc[:, 0:512],
                                in1=sink[:, sl], op=OP.mult)
                return e.tensor_add(out=dst[:, :], in0=dst[:, :],
                                    in1=rsc[:, 0:512])
            b.t('ve', ropek, waits=[('ac', kev)] +
                (rk + rq if (blk == 0 and g == 0) else []), inc='ve')
            for c in range(32):
                tv = b.t('pe', lambda e, c=c, wb=wb: e.matmul(
                    out=ps_mm[1][:],
                    lhsT=wbuf[wb][:, 4096 + c * 128:4096 + (c + 1) * 128],
                    rhs=xnT_blk[:, c * 512:(c + 1) * 512],
                    start=(c == 0), stop=(c == 31)),
                    waits=([('gd', wvl)] + b.now('ve', 'po')) if c == 0 else [],
                    inc='pe' if c == 31 else None)
            vev = b.t('ve', lambda e: e.tensor_copy(out=kvb[:], in_=ps_mm[1][:]),
                      waits=[('pe', tv)] + b.now('po'), inc='ve')
            for q4 in range(4):
                ktile = blk * 4 + q4
                tvt = b.t('pe', lambda e, q4=q4: e.transpose(
                    out=ps_tr[q4 % 2][:], in_=kvb[:, q4 * 128:(q4 + 1) * 128],
                    identity=idbf[:]), waits=[('ve', vev)] + b.now('ve'),
                    inc='pe')
                b.t('ve', lambda e, g=g, ktile=ktile, q4=q4: e.tensor_copy(
                    out=vkm[:, (g * 16 + ktile) * 128:(g * 16 + ktile) * 128 + 128],
                    in_=ps_tr[q4 % 2][:]), waits=[('pe', tvt)], inc='ve')
    kv_ve = b.ctr['ve']

    p2.close()
    p3 = contextlib.ExitStack()
    E3 = lambda n, s, d: p3.enter_context(nc.sbuf_tensor(n, s, d))
    xtileQ = [E3("xtileQ", [128, D], F32)]
    xnbQ = [E3("xnbQ", [128, D], BF16)]
    lnmuQ = E3("lnmuQ", [128, 1], F32)
    lnnvQ = E3("lnnvQ", [128, 1], F32)
    lnrrQ = E3("lnrrQ", [128, 1], F32)
    rscQ = E3("rscQ", [128, 512], BF16)
    wbufQ = [E3("wbufQ0", [128, 32 * 128], BF16)]
    xnoT = E3("xnoT", [128, 32 * 1024], BF16)
    qtmp = E3("qtmp", [128, 512], F32)
    qrb = E3("qrb", [128, 512], BF16)
    qrr = E3("qrr", [128, 512], BF16)
    ps_trQ = [p3.enter_context(nc.psum_tensor(f"ps_trQ{i}", [128, 128], BF16))
             for i in range(2)]
    ps_mmQ = [p3.enter_context(nc.psum_tensor(f"ps_mmQ{i}", [128, 512], F32))
             for i in range(2)]
    _lt = [lnmuQ, lnnvQ, lnrrQ]

    for tt in range(NQT):
        buf = 0
        tl = b.dma('sy', xtileQ[buf][:], x_own[tt * 128:tt * 128 + 128, :],
                   waits=b.now('ve', 'po', 'ac'))
        tln = ln_tile(xtileQ[buf], xnbQ[buf], [('ld', tl)], lnmuQ, lnnvQ, lnrrQ)
        transpose_to(
            xnbQ[buf],
            lambda c, tt=tt: xnoT[:, c * 1024 + tt * 128:c * 1024 + tt * 128 + 128],
            32, [('ve', tln)], ps_trQ)
    xnoT_tok = b.ctr['ve']
    b.dma('sy', xno_hbm, xnoT[:], waits=[('ve', xnoT_tok)], inc='st')

    for h in range(H):
        wb = 0
        wql = b.t('po', lambda e, h=h, wb=wb: e.dma_start(
            out=wbufQ[wb][:, 0:32 * 128].rearrange("p (c n) -> p c n", n=128),
            in_=chunked_wT(wqT)[:, :, h * 128:(h + 1) * 128]),
            waits=b.now('pe'), inc='gd', amt=16)
        for qb in range(2):
            for c in range(32):
                tk = b.t('pe', lambda e, c=c, qb=qb, wb=wb: e.matmul(
                    out=ps_mmQ[qb % 2][:], lhsT=wbufQ[wb][:, c * 128:(c + 1) * 128],
                    rhs=xnoT[:, c * 1024 + qb * 512:c * 1024 + qb * 512 + 512],
                    start=(c == 0), stop=(c == 31)),
                    waits=([('gd', wql), ('ve', xnoT_tok)] + b.now('ac', 've'))
                    if c == 0 else [], inc='pe' if c == 31 else None)
            qev = b.t('ac', lambda e, qb=qb: e.copy(out=qtmp[:],
                                                    in_=ps_mmQ[qb % 2][:]),
                      waits=[('pe', tk)] + b.now('ve', 'po'), inc='ac')
            sl = slice(qb * 512, qb * 512 + 512)

            def ropeq(e, sl=sl):
                e.tensor_copy(out=qrb[:], in_=qtmp[:])
                e.tensor_scalar(out=rscQ[0:64, 0:512], in0=qrb[64:128, :],
                                scalar1=-1.0, scalar2=None, op0=OP.mult)
                e.tensor_scalar(out=rscQ[64:128, 0:512], in0=qrb[0:64, :],
                                scalar1=1.0, scalar2=None, op0=OP.mult)
                e.tensor_tensor(out=qrr[:, :], in0=qrb[:], in1=cosq[:, sl],
                                op=OP.mult)
                e.tensor_tensor(out=rscQ[:, 0:512], in0=rscQ[:, 0:512],
                                in1=sinq[:, sl], op=OP.mult)
                return e.tensor_add(out=qrr[:, :], in0=qrr[:, :],
                                    in1=rscQ[:, 0:512])
            trq = b.t('ve', ropeq, waits=[('ac', qev), ('st', b.ctr['st'])],
                      inc='ve')
            b.t('sy', lambda e, h=h, qb=qb: e.dma_start(
                out=qr_hbm[h, qb], in_=qrr[:]),
                waits=[('ve', trq)], inc='st', amt=16)

    q_st = b.ctr['st']

    p3.close()

    p4 = contextlib.ExitStack()
    E4 = lambda n, s, d: p4.enter_context(nc.sbuf_tensor(n, s, d))
    qr_hg = E4("qr_hg", [128, G * 1024], BF16)
    sbA = E4("sbA", [128, G * LS[3]], BF16)
    sbB = E4("sbB", [128, G * LS[7]], BF16)
    scrV = E4("scrV", [128, 2048], BF16)
    scrP = E4("scrP", [128, 2048], BF16)
    scrA = E4("scrA", [128, 2048], BF16)
    ubuf = E4("ubuf", [128, 2048], F32)
    u2buf = E4("u2buf", [128, 2048], F32)
    ebuf = E4("ebuf", [128, 256], BF16)
    ptbuf = E4("ptbuf", [128, 256], BF16)
    attb = E4("attb", [128, 256], BF16)
    zbf = E4("zbf", [128, 256], F32)
    ps_sq = [p4.enter_context(nc.psum_tensor(f"ps_sq{i}", [128, 512], F32))
             for i in range(2)]
    ps_st = [p4.enter_context(nc.psum_tensor(f"ps_st{i}", [128, 256], F32))
             for i in range(2)]
    ps_av = p4.enter_context(nc.psum_tensor("ps_av", [128, 256], F32))
    ps_z = p4.enter_context(nc.psum_tensor("ps_z", [1, 256], F32))
    ps_zb = p4.enter_context(nc.psum_tensor("ps_zb", [128, 256], F32))
    ps_t4 = p4.enter_context(nc.psum_tensor("ps_t4", [4, 128], BF16))

    def sb_ap(sp, hh, j):
        slab = sbA if sp == 0 else sbB
        return slab[:, hh * (LS[3] if sp == 0 else LS[7]):
                    hh * (LS[3] if sp == 0 else LS[7]) + LS[j]]

    for hg in range(NG):
        h0 = hg * G
        qld = []
        for hh in range(G):
            for qb in range(2):
                t_ = b.dma('sy',
                           qr_hg[:, hh * 1024 + qb * 512: hh * 1024 + qb * 512 + 512],
                           qr_hbm[h0 + hh, qb],
                           waits=[('st', q_st)] + b.now('pe'), inc='lq')
            qld.append(t_)

        for batch in range(4):
            slots = (batch, batch + 4)
            for sp, j in enumerate(slots):
                for hh in range(G):
                    g = (h0 + hh) // REP
                    nkb = cdiv(LS[j], 512)
                    for kb in range(nkb):
                        kw = min(512, LS[j] - kb * 512)
                        pb = (kb + sp * 2) % 2
                        tk = b.t('pe', lambda e, hh=hh, j=j, kb=kb, kw=kw, pb=pb, g=g: e.matmul(
                            out=ps_sq[pb][:, 0:kw],
                            lhsT=qr_hg[:, hh * 1024 + j * 128: hh * 1024 + j * 128 + 128],
                            rhs=krT[:, g * S + kb * 512: g * S + kb * 512 + kw],
                            start=True, stop=True),
                            waits=[('lq', qld[hh]), ('ve', kv_ve)] +
                            b.now('ac', 've', 'po'), inc='pe')
                        b.t('ac', lambda e, sp=sp, hh=hh, j=j, kb=kb, kw=kw, pb=pb: e.copy(
                            out=sb_ap(sp, hh, j)[:, kb * 512: kb * 512 + kw],
                            in_=ps_sq[pb][:, 0:kw]),
                            waits=[('pe', tk), ('gd', a2)], inc='ac')
                    b.t('ve', lambda e, sp=sp, hh=hh, j=j: e.tensor_tensor(
                        out=sb_ap(sp, hh, j)[:, LS[j] - 256: LS[j]],
                        in0=sb_ap(sp, hh, j)[:, LS[j] - 256: LS[j]],
                        in1=sbmasks[:, j * 256:(j + 1) * 256],
                        op=OP.add), waits=b.now('ac'), inc='ve')
                    col = sp * G + hh
                    def sig(e, sp=sp, hh=hh, j=j, col=col):
                        return e.scalar_tensor_tensor(
                            out=scrV[:, 0:128], in0=sb_ap(sp, hh, j)[:, 0:128],
                            scalar=1.0, in1=sb_ap(sp, hh, j)[:, 0:128],
                            op0=OP.mult, op1=OP.mult,
                            accum_out=ST['CNT'][:, col:col + 1])
                    b.t('ve', sig, inc='ve')
            sb_ac = b.ctr['ac']
            sb_ve = b.ctr['ve']

            b.t('ve', lambda e: e.tensor_scalar(
                out=ST['CNT'][:], in0=ST['CNT'][:], scalar1=1.0 / 128,
                scalar2=None, op0=OP.mult), waits=[('ve', sb_ve), ('ld', a0)],
                inc='ve')
            tqq = b.t('ac', lambda e: e.activation(out=ST['F'][:],
                                                   in_=ST['CNT'][:],
                                                   func=AF.Sqrt),
                      waits=b.now('ve'), inc='ac')

            ti1 = b.t('ve', lambda e: e.tensor_scalar(
                out=ST['F'][:], in0=ST['F'][:], scalar1=SIGCLAMP,
                scalar2=None, op0=OP.min), waits=[('ac', tqq)], inc='ve')

            def init2(e, slots=slots):
                e.tensor_scalar(out=ST['LO'][:], in0=ST['F'][:], scalar1=-8.0,
                                scalar2=-1.0, op0=OP.mult, op1=OP.add)
                e.tensor_scalar(out=ST['HI'][:], in0=ST['F'][:], scalar1=8.0,
                                scalar2=1.0, op0=OP.mult, op1=OP.add)
                for sp, j in enumerate(slots):
                    sl = slice(sp * G, sp * G + G)
                    e.tensor_scalar(out=ST['NLO'][:, sl],
                                    in0=SL[j]['NQ'].to_broadcast([128, G]),
                                    scalar1=1.0, scalar2=None, op0=OP.mult)
                return e.memset(ST['NHI'][:], 0.0)
            ti2 = b.t('ve', init2, waits=[('ve', ti1)], inc='ve')

            def init3(e, slots=slots, hg=hg, h0=h0):
                for sp, j in enumerate(slots):
                    for hh in range(G):
                        h = h0 + hh
                        col = sp * G + hh
                        dst = ST['T'][:, col:col + 1]
                        if j in (0, 4):
                            if hg == 0:
                                e.tensor_scalar(out=dst, in0=ST['HI'][:, col:col + 1],
                                                scalar1=0.15, scalar2=None,
                                                op0=OP.mult)
                            else:
                                e.tensor_copy(out=dst,
                                              in_=taustate[:, (h - G) * 8 + j:
                                                           (h - G) * 8 + j + 1])
                        else:
                            e.tensor_copy(out=dst,
                                          in_=taustate[:, h * 8 + j - 1:
                                                       h * 8 + j])
                return e.tensor_scalar(out=ST['SC1'][:], in0=ST['HI'][:],
                                       scalar1=-1e-3, scalar2=None, op0=OP.add)
            ti3 = b.t('ve', init3, waits=[('ve', ti2)], inc='ve')

            def init4(e):
                e.tensor_tensor(out=ST['T'][:], in0=ST['T'][:], in1=ST['SC1'][:],
                                op=OP.min)
                return e.tensor_scalar(out=ST['SC2'][:], in0=ST['LO'][:],
                                       scalar1=1e-3, scalar2=None, op0=OP.add)
            ti4 = b.t('ve', init4, waits=[('ve', ti3)], inc='ve')
            tprev = b.t('ve', lambda e: e.tensor_tensor(
                out=ST['T'][:], in0=ST['T'][:], in1=ST['SC2'][:], op=OP.max),
                waits=[('ve', ti4)], inc='ve')

            for it in range(NPROBE):
                cw = []
                for sp, j in enumerate(slots):
                    for hh in range(G):
                        col = sp * G + hh
                        lane = (col + it) % 8
                        eng = 've' if lane < 6 else 'ac'
                        if eng != 'ac':
                            scr = scrV
                            tk = b.t(eng, lambda e, sp=sp, hh=hh, j=j, col=col, scr=scr: e.tensor_scalar(
                                out=scr[:, 0:LS[j]], in0=sb_ap(sp, hh, j)[:],
                                scalar1=ST['T'][:, col:col + 1], scalar2=None,
                                op0=OP.is_gt, op1=OP.add,
                                accum_out=ST['CNT'][:, col:col + 1]),
                                waits=[('ve', tprev), ('ac', sb_ac)], inc=eng)
                            cw.append((eng, tk))
                        else:
                            tn = b.t('ve', lambda e, col=col: e.tensor_scalar(
                                out=ST['SC2'][:, col:col + 1],
                                in0=ST['T'][:, col:col + 1], scalar1=-1.0,
                                scalar2=None, op0=OP.mult),
                                waits=[('ve', tprev)], inc='ve')
                            ta = b.t('ac', lambda e, sp=sp, hh=hh, j=j, col=col: e.activation(
                                out=scrA[:, 0:LS[j]], in_=sb_ap(sp, hh, j)[:],
                                func=AF.Sign, bias=ST['SC2'][:, col:col + 1],
                                accum_out=ST['CNT'][:, col:col + 1]),
                                waits=[('ve', tn), ('ac', sb_ac)], inc='ac')
                            tb = b.t('ve', lambda e, col=col: e.tensor_scalar(
                                out=ST['CNT'][:, col:col + 1],
                                in0=ST['CNT'][:, col:col + 1], scalar1=0.5,
                                scalar2=None, op0=OP.mult),
                                waits=[('ac', ta)], inc='ve')
                            tb = b.t('ve', lambda e, col=col, j=j: e.scalar_tensor_tensor(
                                out=ST['CNT'][:, col:col + 1], in0=SL[j]['LP'],
                                scalar=0.5, in1=ST['CNT'][:, col:col + 1],
                                op0=OP.mult, op1=OP.add),
                                waits=[('ve', tb)], inc='ve')
                            cw.append(('ve', tb))

                def bk1(e, slots=slots):
                    for sp, j in enumerate(slots):
                        sl = slice(sp * G, sp * G + G)
                        e.tensor_scalar(out=STU['GE'][:, sl], in0=ST['CNT'][:, sl],
                                        scalar1=SL[j]['TGT'], scalar2=None,
                                        op0=OP.is_ge)
                        e.tensor_scalar(out=STU['NE'][:, sl], in0=ST['CNT'][:, sl],
                                        scalar1=SL[j]['TGT'], scalar2=None,
                                        op0=OP.is_lt)
                    return e.engine_nop()
                tbk1 = b.t('ve', bk1, waits=cw, inc='ve')

                def bk2(e):
                    e.copy_predicated(out=ST['LO'][:], mask=STU['GE'][:],
                                      data=ST['T'][:])
                    e.copy_predicated(out=ST['NLO'][:], mask=STU['GE'][:],
                                      data=ST['CNT'][:])
                    e.copy_predicated(out=ST['HI'][:], mask=STU['NE'][:],
                                      data=ST['T'][:])
                    return e.copy_predicated(out=ST['NHI'][:], mask=STU['NE'][:],
                                             data=ST['CNT'][:])
                tbk2 = b.t('ve', bk2, waits=[('ve', tbk1)], inc='ve')
                if it == NPROBE - 1:
                    tprev = tbk2
                    continue

                def bk3(e, slots=slots, it=it):
                    e.scalar_tensor_tensor(out=ST['SC2'][:], in0=ST['NHI'][:],
                                           scalar=-1.0, in1=ST['NLO'][:],
                                           op0=OP.mult, op1=OP.add)
                    for sp, j in enumerate(slots):
                        sl = slice(sp * G, sp * G + G)
                        e.tensor_scalar(out=ST['SC1'][:, sl],
                                        in0=ST['NLO'][:, sl],
                                        scalar1=SL[j]['AIM'], scalar2=None,
                                        op0=OP.subtract)
                    for sp, j in enumerate(slots):
                        sl = slice(sp * G, sp * G + G)
                        e.tensor_scalar(out=STU['UB'][:, sl],
                                        in0=ST['NHI'][:, sl],
                                        scalar1=SL[j]['T32'], scalar2=None,
                                        op0=OP.is_ge)
                    e.tensor_tensor(out=ST['JC'][:], in0=ST['HI'][:],
                                    in1=ST['LO'][:], op=OP.subtract)
                    if it % 2 == 1:
                        for sp, j in enumerate(slots):
                            sl = slice(sp * G, sp * G + G)
                            e.tensor_scalar(out=STU['UA'][:, sl],
                                            in0=ST['NHI'][:, sl],
                                            scalar1=SL[j]['T64A'],
                                            scalar2=None, op0=OP.is_lt)
                            e.tensor_scalar(out=STU['GE'][:, sl],
                                            in0=ST['NLO'][:, sl],
                                            scalar1=SL[j]['T64B'],
                                            scalar2=None, op0=OP.is_gt)
                        e.memset(ST['TAU'][:], 0.5)
                    return e.tensor_scalar(out=ST['SC2'][:], in0=ST['SC2'][:],
                                           scalar1=1.0, scalar2=None, op0=OP.max)
                tbk3 = b.t('ve', bk3, waits=[('ve', tbk2)], inc='ve')

                def bk4(e, it=it):
                    r = e.reciprocal(out=ST['SC2'][:], in_=ST['SC2'][:])
                    if it % 2 == 1:
                        r = e.tensor_tensor(out=STU['UA'][:], in0=STU['UA'][:],
                                            in1=STU['GE'][:], op=OP.bitwise_or)
                    return r
                tbk4 = b.t('ve', bk4, waits=[('ve', tbk3)], inc='ve')
                tbk5 = b.t('ve', lambda e: e.tensor_tensor(
                    out=ST['F'][:], in0=ST['SC1'][:], in1=ST['SC2'][:],
                    op=OP.mult), waits=[('ve', tbk4)], inc='ve')
                tbk6 = b.t('ve', lambda e: e.tensor_scalar(
                    out=ST['SC1'][:], in0=ST['F'][:], scalar1=0.5, scalar2=0.25,
                    op0=OP.mult, op1=OP.add), waits=[('ve', tbk5)], inc='ve')
                tbk7 = b.t('ve', lambda e: e.copy_predicated(
                    out=ST['SC1'][:], mask=STU['UB'][:], data=ST['F'][:]),
                    waits=[('ve', tbk6)], inc='ve')
                if it % 2 == 1:
                    tbk7 = b.t('ve', lambda e: e.copy_predicated(
                        out=ST['SC1'][:], mask=STU['UA'][:], data=ST['TAU'][:]),
                        waits=[('ve', tbk7)], inc='ve')
                tbk8 = b.t('ve', lambda e: e.tensor_scalar(
                    out=ST['SC1'][:], in0=ST['SC1'][:], scalar1=0.06,
                    scalar2=0.94, op0=OP.max, op1=OP.min),
                    waits=[('ve', tbk7)], inc='ve')
                tbk9 = b.t('ve', lambda e: e.tensor_tensor(
                    out=ST['SC2'][:], in0=ST['JC'][:], in1=ST['SC1'][:],
                    op=OP.mult), waits=[('ve', tbk8)], inc='ve')
                tprev = b.t('ve', lambda e: e.tensor_add(
                    out=ST['T'][:], in0=ST['LO'][:], in1=ST['SC2'][:]),
                    waits=[('ve', tbk9)], inc='ve')

            fw = []
            prev_tu2 = None
            for sp, j in enumerate(slots):
                for hh in range(G):
                    col = sp * G + hh
                    tu = b.t('ve', lambda e, sp=sp, hh=hh, j=j, col=col: e.tensor_scalar(
                        out=ubuf[:, 0:LS[j]], in0=sb_ap(sp, hh, j)[:],
                        scalar1=ST['LO'][:, col:col + 1], scalar2=0.0,
                        op0=OP.subtract, op1=OP.max),
                        waits=[('ve', tprev)] + ([('ve', prev_tu2)] if prev_tu2 else []),
                        inc='ve')
                    twd = b.t('ve', lambda e, col=col: e.tensor_tensor(
                        out=ST['SC1'][:, col:col + 1],
                        in0=ST['HI'][:, col:col + 1],
                        in1=ST['LO'][:, col:col + 1], op=OP.subtract), inc='ve')
                    tu2 = b.t('ve', lambda e, j=j, col=col: e.scalar_tensor_tensor(
                        out=u2buf[:, 0:LS[j]], in0=ubuf[:, 0:LS[j]],
                        scalar=ST['SC1'][:, col:col + 1], in1=ubuf[:, 0:LS[j]],
                        op0=OP.is_le, op1=OP.mult),
                        waits=[('ve', tu), ('ve', twd)], inc='ve')
                    t8 = b.t('ve', lambda e, col=col, j=j: e.max(
                        out=w8[:, col * 8:col * 8 + 8], in_=u2buf[:, 0:LS[j]]),
                        waits=[('ve', tu2)], inc='ve')
                    prev_tu2 = tu2
                    fw.append(('ve', t8))

            def fn1(e, slots=slots):
                for sp, j in enumerate(slots):
                    sl = slice(sp * G, sp * G + G)
                    e.tensor_scalar(out=ST['F'][:, sl], in0=ST['NHI'][:, sl],
                                    scalar1=SL[j]['TGT'], scalar2=-1.0,
                                    op0=OP.subtract, op1=OP.mult)
                return e.engine_nop()
            tf1 = b.t('ve', fn1, waits=fw, inc='ve')

            def fn2(e):
                e.tensor_scalar(out=ST['JC'][:], in0=ST['F'][:], scalar1=1.0,
                                scalar2=0.0, op0=OP.subtract, op1=OP.max)
                e.tensor_scalar(out=STU['UA'][:], in0=ST['F'][:], scalar1=1.0,
                                scalar2=None, op0=OP.is_lt)
                e.tensor_scalar(out=STU['UB'][:], in0=ST['F'][:], scalar1=7.0,
                                scalar2=None, op0=OP.is_gt)
                return e.tensor_scalar(out=ST['SC1'][:], in0=ST['LO'][:],
                                       scalar1=-1.0, scalar2=None, op0=OP.mult)
            tf2 = b.t('ve', fn2, waits=[('ve', tf1)], inc='ve')

            def fn3(e, slots=slots):
                e.tensor_scalar(out=ST['JC'][:], in0=ST['JC'][:], scalar1=6.0,
                                scalar2=None, op0=OP.min)
                e.tensor_tensor(out=STU['UA'][:], in0=STU['UA'][:],
                                in1=STU['UB'][:], op=OP.bitwise_or)
                e.tensor_tensor(out=ST['SC1'][:], in0=ST['SC1'][:],
                                in1=ST['LO'][:], op=OP.max)
                for sp, j in enumerate(slots):
                    sl = slice(sp * G, sp * G + G)
                    e.tensor_scalar(out=STU['NE'][:, sl], in0=ST['NLO'][:, sl],
                                    scalar1=SL[j]['TGT'], scalar2=None,
                                    op0=OP.is_equal)
                return e.memset(ST['SC2'][:], 1e-30)
            tf3 = b.t('ve', fn3, waits=[('ve', tf2)], inc='ve')

            def fn4(e):
                e.tensor_scalar(out=ST['F'][:], in0=ST['JC'][:], scalar1=1.0,
                                scalar2=None, op0=OP.add)
                for col in range(2 * G):
                    e.tensor_scalar(out=ubuf[:, col * 8: col * 8 + 8],
                                    in0=iota8s[:],
                                    scalar1=ST['JC'][:, col:col + 1],
                                    scalar2=None, op0=OP.is_equal)
                return e.tensor_scalar(out=ST['SC1'][:], in0=ST['SC1'][:],
                                       scalar1=-0.00390625, scalar2=-1e-6,
                                       op0=OP.mult, op1=OP.add)
            tf4 = b.t('ve', fn4, waits=[('ve', tf3)], inc='ve')

            def fn5(e):
                for col in range(2 * G):
                    e.tensor_scalar(out=u2buf[:, col * 8: col * 8 + 8],
                                    in0=iota8s[:],
                                    scalar1=ST['F'][:, col:col + 1],
                                    scalar2=None, op0=OP.is_equal)
                return e.copy_predicated(out=ST['SC1'][:], mask=STU['NE'][:],
                                         data=ST['SC2'][:])
            tf5 = b.t('ve', fn5, waits=[('ve', tf4)], inc='ve')

            tf6 = b.t('ve', lambda e: e.tensor_add(
                out=ubuf[:, 0:2 * G * 8], in0=ubuf[:, 0:2 * G * 8],
                in1=u2buf[:, 0:2 * G * 8]), waits=[('ve', tf5)], inc='ve')

            def fn7(e):
                for col in range(2 * G):
                    e.scalar_tensor_tensor(out=u2buf[:, col * 8: col * 8 + 8],
                                           in0=ubuf[:, col * 8: col * 8 + 8],
                                           scalar=0.5,
                                           in1=w8[:, col * 8:col * 8 + 8],
                                           op0=OP.mult, op1=OP.mult,
                                           accum_out=ST['TAU'][:, col:col + 1])
                return e.engine_nop()
            tf7 = b.t('ve', fn7, waits=[('ve', tf6)], inc='ve')

            tf8 = b.t('ve', lambda e: e.copy_predicated(
                out=ST['TAU'][:], mask=STU['UA'][:], data=ST['SC1'][:]),
                waits=[('ve', tf7)], inc='ve')

            def fn9(e, slots=slots):
                for sp, j in enumerate(slots):
                    sl = slice(sp * G, sp * G + G)
                    e.scalar_tensor_tensor(
                        out=ST['TAU'][:, sl],
                        in0=SL[j]['KEEP'].to_broadcast([128, G]), scalar=-1.0,
                        in1=ST['TAU'][:, sl], op0=OP.mult, op1=OP.add)
                return e.engine_nop()
            tf9 = b.t('ve', fn9, waits=[('ve', tf8)], inc='ve')

            def fn10(e, slots=slots, h0=h0):
                e.tensor_scalar(out=ST['SC1'][:], in0=ST['LO'][:], scalar1=-1.0,
                                scalar2=None, op0=OP.mult)
                e.tensor_scalar(out=ST['JC'][:], in0=ST['TAU'][:], scalar1=-1.0,
                                scalar2=None, op0=OP.mult)
                for sp, j in enumerate(slots):
                    for hh in range(G):
                        col = sp * G + hh
                        e.tensor_tensor(
                            out=taustate[:, (h0 + hh) * 8 + j:
                                         (h0 + hh) * 8 + j + 1],
                            in0=ST['LO'][:, col:col + 1],
                            in1=ST['TAU'][:, col:col + 1], op=OP.add)
                e.tensor_copy(out=cpk[:, 0:2 * G * 4:4], in_=ST['SC1'][:])
                return e.tensor_copy(out=cpk[:, 2:2 * G * 4:4], in_=ST['JC'][:])
            tf10 = b.t('ve', fn10, waits=[('ve', tf9)], inc='ve')

            def fn11(e):
                e.tensor_copy(out=ST['SC2'][:], in_=cpk[:, 0:2 * G * 4:4])
                return e.tensor_copy(out=ST['F'][:], in_=cpk[:, 2:2 * G * 4:4])
            tf11 = b.t('ve', fn11, waits=[('ve', tf10)], inc='ve')

            def fn12(e):
                e.tensor_tensor(out=ST['SC2'][:], in0=ST['SC1'][:],
                                in1=ST['SC2'][:], op=OP.subtract)
                return e.tensor_tensor(out=ST['F'][:], in0=ST['JC'][:],
                                       in1=ST['F'][:], op=OP.subtract)
            tf12 = b.t('ve', fn12, waits=[('ve', tf11)], inc='ve')

            def fn13(e):
                e.tensor_copy(out=cpk[:, 1:2 * G * 4:4], in_=ST['SC2'][:])
                return e.tensor_copy(out=cpk[:, 3:2 * G * 4:4], in_=ST['F'][:])
            tfin = b.t('ve', fn13, waits=[('ve', tf12)], inc='ve')

            for sp, j in enumerate(slots):
                for hh in range(G):
                    col = sp * G + hh
                    tk = b.t('pe', lambda e, col=col: e.transpose(
                        out=ps_t4[:], in_=cpk[:, col * 4:col * 4 + 4],
                        identity=idbf[:]),
                        waits=[('ve', tfin)] + b.now('ve'), inc='pe')
                    b.t('ve', lambda e, j=j, hh=hh: e.tensor_copy(
                        out=rrows[0:4, (j * G + hh) * 128:(j * G + hh) * 128 + 128],
                        in_=ps_t4[:]), waits=[('pe', tk)], inc='ve')

            if batch in (1, 3):
                pairs = (0, 2) if batch == 1 else (1, 3)
                for cp in pairs:
                    nkt = 4 * cp + 4
                    qcol = 2 * cp * 128
                    for hh in range(G):
                        h = h0 + hh
                        g = h // REP
                        rr_ap = rrows[0:4, :].rearrange(
                            "p (j g q) -> p j g q", g=G, q=128)[
                            :, 2 * cp:2 * cp + 2, hh, :]
                        prev_pt = None
                        prev_zt = None
                        for t in range(nkt):
                            pb = t % 2
                            tm1 = b.t('pe', lambda e, t=t, hh=hh, g=g, pb=pb, qcol=qcol: e.matmul(
                                out=ps_st[pb][:],
                                lhsT=krT[:, g * S + t * 128: g * S + t * 128 + 128],
                                rhs=qr_hg[:, hh * 1024 + qcol: hh * 1024 + qcol + 256],
                                start=True, stop=False),
                                waits=b.now('ac', 've'), inc='pe')
                            tm2 = b.t('pe', lambda e, pb=pb, rr_ap=rr_ap: e.matmul(
                                out=ps_st[pb][:], lhsT=ones4r[:], rhs=rr_ap,
                                start=False, stop=True), inc='pe')
                            tev = b.t('ac', lambda e, pb=pb: e.activation(
                                out=ebuf[:], in_=ps_st[pb][:], func=AF.Exp,
                                scale=ISQ),
                                waits=[('pe', tm2)] +
                                ([prev_pt] if prev_pt else []), inc='ac')
                            li = t - 4 * cp
                            war = [prev_zt] if prev_zt else []
                            if li >= 0:
                                tpt = b.t('ve', lambda e: e.scalar_tensor_tensor(
                                    out=ptbuf[:], in0=ebuf[:], scalar=1.0,
                                    in1=ebuf[:], op0=OP.is_ge, op1=OP.mult),
                                    waits=[('ac', tev)] + war, inc='ve')
                                tpt = b.t('po', lambda e, cp=cp, li=li: e.tensor_tensor(
                                    out=ptbuf[:], in0=ptbuf[:],
                                    in1=ptmasks[:, (cp * 4 + li) * 256:
                                                (cp * 4 + li) * 256 + 256],
                                    op=OP.mult),
                                    waits=[('ve', tpt), ('gd', a3)], inc='po')
                                ptw = ('po', tpt)
                            else:
                                tpt = b.t('ve', lambda e: e.scalar_tensor_tensor(
                                    out=ptbuf[:], in0=ebuf[:], scalar=1.0,
                                    in1=ebuf[:], op0=OP.is_ge, op1=OP.mult),
                                    waits=[('ac', tev)] + war, inc='ve')
                                ptw = ('ve', tpt)
                            prev_pt = ptw
                            tav = b.t('pe', lambda e, t=t, g=g, nkt=nkt: e.matmul(
                                out=ps_av[:],
                                lhsT=vkm[:, (g * 16 + t) * 128:(g * 16 + t) * 128 + 128],
                                rhs=ptbuf[:], start=(t == 0), stop=(t == nkt - 1)),
                                waits=[ptw], inc='pe')
                            tz = b.t('pe', lambda e, t=t, nkt=nkt: e.matmul(
                                out=ps_z[:], lhsT=ones128[:], rhs=ptbuf[:],
                                start=(t == 0), stop=(t == nkt - 1)),
                                inc='pe')
                            prev_zt = ('pe', tz)
                        tz1 = b.t('ve', lambda e: e.tensor_copy(out=zrow[:],
                                                                in_=ps_z[:]),
                                  waits=[('pe', tz)], inc='ve')
                        tz2 = b.t('ve', lambda e: e.reciprocal(out=zinv[:],
                                                               in_=zrow[:]),
                                  inc='ve')
                        tz3 = b.t('sy', lambda e: e.dma_start(
                            out=zdram, in_=zinv[:]),
                            waits=[('ve', tz2)], inc='st', amt=16)
                        tz4 = b.t('sy', lambda e: e.dma_start(
                            out=zbf[:],
                            in_=zdram.rearrange("p (o n) -> p o n", o=1)
                            .to_broadcast([1, 128, 256])),
                            waits=[('st', tz3), ('ve', b.ctr['ve'])],
                            inc='ld', amt=16)
                        tae = b.t('ve', lambda e: e.tensor_tensor(
                            out=attb[:], in0=ps_av[:], in1=zbf[:], op=OP.mult),
                            waits=[('pe', tav), ('ld', tz4), ('st', b.ctr['st'])],
                            inc='ve')
                        b.t('sy', lambda e, h=h, qcol=qcol: e.dma_start(
                            out=attn_hbm[h, :, qcol:qcol + 256], in_=attb[:]),
                            waits=[('ve', tae)], inc='st', amt=16)
    att_st = b.ctr['st']
    p4.close()
    pA.close()

    p5 = contextlib.ExitStack()
    E5 = lambda n, s, d: p5.enter_context(nc.sbuf_tensor(n, s, d))
    biasb = E5("biasb", [1, D], BF16)
    w128 = [E5(f"w128_{i}", [128, 1], F32) for i in range(2)]
    cs16 = E5("cs16", [16, 1], F32)
    ss16 = E5("ss16", [16, 1], F32)
    w16 = E5("w16", [16, 1], F32)
    a4 = b.t('po', lambda e: e.dma_start(out=biasb[:], in_=fbias),
             waits=b.now('pe', 've'), inc='gd', amt=16)
    wl1 = b.dma('sy', cs16[:], coeffs)
    wl1 = b.dma('sy', ss16[:], scales)
    ww = b.t('ve', lambda e: e.tensor_tensor(out=w16[:], in0=cs16[:],
                                             in1=ss16[:], op=OP.mult),
             waits=[('ld', wl1)], inc='ve')
    ww2 = b.t('sy', lambda e: e.dma_start(out=wdram, in_=w16[:]),
              waits=[('ve', ww)], inc='st', amt=16)
    for eg in range(2):
        b.t('sy', lambda e, eg=eg: e.dma_start(
            out=w128[eg][:],
            in_=wdram[eg * 8:eg * 8 + 8].to_broadcast([8, 16, 1])),
            waits=[('st', ww2)], inc='ld', amt=16)
    xnoT2 = E5("xnoT2", [128, 32 * 1024], BF16)
    wblk = [E5(f"wblk{i}", [128, 32 * 512], BF16) for i in range(2)]
    bfb = E5("bfb", [128, 2 * D], BF16)
    xab = E5("xab", [128, 2 * 1024], BF16)
    atile = [E5(f"atile{i}", [128, 32 * 128], BF16) for i in range(2)]
    xot = [E5(f"xot{i}", [128, 512], F32) for i in range(2)]
    osb = [E5(f"osb{i}", [128, 512], F32) for i in range(2)]
    ps5 = [p5.enter_context(nc.psum_tensor(f"ps5_{i}", [128, 512], F32))
           for i in range(8)]

    xrl = b.dma('sy', xnoT2[:], xno_hbm, waits=b.now('pe', 've'))
    bfl = b.t('po', lambda e: e.dma_start(
        out=bfb[:].rearrange("p (g n) -> p g n", g=2),
        in_=bf_st.rearrange("(g p) n -> p g n", p=128)), inc='gd', amt=16)
    asl = b.t('po', lambda e: e.dma_start(
        out=wblk[0][:, 0:32 * 256].rearrange("p (c n) -> p c n", n=256),
        in_=chunked_wT(a_st)), inc='gd', amt=16)

    for eg in range(2):
        for tb in range(2):
            for c in range(32):
                tk = b.t('pe', lambda e, eg=eg, tb=tb, c=c: e.matmul(
                    out=ps5[0][:],
                    lhsT=wblk[0][:, c * 256 + eg * 128: c * 256 + eg * 128 + 128],
                    rhs=xnoT2[:, c * 1024 + tb * 512: c * 1024 + tb * 512 + 512],
                    start=(c == 0), stop=(c == 31)),
                    waits=([('gd', asl), ('ld', xrl), ('ld', b.ctr['ld'])]
                           + b.now('ve'))
                    if c == 0 else [], inc='pe' if c == 31 else None)
            b.t('ve', lambda e, eg=eg, tb=tb: e.tensor_scalar(
                out=xab[:, eg * 1024 + tb * 512: eg * 1024 + tb * 512 + 512],
                in0=ps5[0][:], scalar1=w128[eg][:], scalar2=None,
                op0=OP.mult), waits=[('pe', tk)], inc='ve')
    xa_ve = b.ctr['ve']

    for ob in range(8):
        wb = ob % 2
        osl = slice(ob * 512, ob * 512 + 512)
        wol = b.t('po', lambda e, wb=wb, osl=osl: e.dma_start(
            out=wblk[wb][:].rearrange("p (c n) -> p c n", n=512),
            in_=chunked_wT(woT)[:, :, osl]),
            waits=b.now('pe') + [('ve', xa_ve)], inc='gd', amt=16)
        for tt in range(8):
            ab = tt % 2
            tal = b.t('sy', lambda e, tt=tt, ab=ab: e.dma_start(
                out=atile[ab][:].rearrange("p (h n) -> p h n", n=128),
                in_=attn_hbm.rearrange("h p n -> p h n")[:, :, tt * 128:tt * 128 + 128]),
                waits=[('st', att_st)] + b.now('pe'), inc='la', amt=16)
            for hh in range(H):
                tk = b.t('pe', lambda e, tt=tt, hh=hh, wb=wb, ab=ab: e.matmul(
                    out=ps5[tt][:], lhsT=atile[ab][:, hh * 128:(hh + 1) * 128],
                    rhs=wblk[wb][:, hh * 512:(hh + 1) * 512],
                    start=(hh == 0), stop=False),
                    waits=[('la', tal), ('gd', wol)] + b.now('ve')
                    if hh == 0 else [], inc=None)
            b.t('pe', lambda e, tt=tt, wb=wb, ob=ob: e.matmul(
                out=ps5[tt][:], lhsT=ones1r[:],
                rhs=biasb[:, ob * 512:ob * 512 + 512], start=False, stop=False),
                waits=[('gd', a4)], inc='pe')
        wfl = b.t('po', lambda e, wb=wb, osl=osl: e.dma_start(
            out=wblk[1 - wb][:].rearrange("p (c n) -> p c n", n=512),
            in_=chunked_wT(wT)[:, :, osl]), waits=b.now('pe'), inc='gd', amt=16)
        for tt in range(8):
            xol = b.dma('sy', xot[tt % 2][:],
                        x_own[tt * 128:tt * 128 + 128, osl],
                        waits=b.now('ve'), inc='lx')
            for c in range(32):
                tk = b.t('pe', lambda e, tt=tt, c=c, wb=wb: e.matmul(
                    out=ps5[tt][:], lhsT=xnoT2[:, c * 1024 + tt * 128:
                                               c * 1024 + tt * 128 + 128],
                    rhs=wblk[1 - wb][:, c * 512:(c + 1) * 512],
                    start=False, stop=False),
                    waits=[('gd', wfl)] if c == 0 else [], inc=None)
            for eg in range(2):
                tk = b.t('pe', lambda e, tt=tt, eg=eg, ob=ob: e.matmul(
                    out=ps5[tt][:],
                    lhsT=xab[:, eg * 1024 + tt * 128: eg * 1024 + tt * 128 + 128],
                    rhs=bfb[:, eg * D + ob * 512: eg * D + ob * 512 + 512],
                    start=False, stop=(eg == 1)),
                    waits=[('gd', bfl), ('ve', xa_ve)] if eg == 0 else [],
                    inc='pe' if eg == 1 else None)
            tev = b.t('ve', lambda e, tt=tt: e.tensor_tensor(
                out=osb[tt % 2][:], in0=ps5[tt][:], in1=xot[tt % 2][:],
                op=OP.add), waits=[('pe', tk), ('lx', xol),
                                   ('st', b.ctr['st'])], inc='ve')
            b.t('sy', lambda e, tt=tt, osl=osl: e.dma_start(
                out=out_own[tt * 128:tt * 128 + 128, osl],
                in_=osb[tt % 2][:]), waits=[('ve', tev)], inc='st', amt=16)

    final_st = b.ctr['st']
    b.t('sy', lambda e: None, waits=[('st', final_st)])

    with nc.Block() as block:
        b.emit(block)
    p5.close()
    b.es.close()
    return nc


_PROG = None


def _host_aux(p):
    tiles = [2 * j + p for j in range(NSLOT)]
    qtrue = np.concatenate([np.arange(t * 128, t * 128 + 128) for t in tiles])
    selc = np.zeros((128, NSLOT * 8), np.float32)
    for j, t in enumerate(tiles):
        q = np.arange(t * 128, t * 128 + 128)
        nq = (q + 1).astype(np.float32)
        tgt = np.minimum(nq, TGT)
        col = j * 8
        selc[:, col + 0] = tgt
        selc[:, col + 1] = tgt - 4.0
        selc[:, col + 2] = tgt - 32.0
        selc[:, col + 3] = tgt - 64.0
        selc[:, col + 4] = tgt + 64.0
        selc[:, col + 5] = (tgt >= nq).astype(np.float32)
        selc[:, col + 6] = float(LS[j])
        selc[:, col + 7] = nq
    sbm = np.zeros((128, NSLOT * 256), np.float32)
    for j, t in enumerate(tiles):
        kbase = (KT[j] - 2) * 128
        for r in range(128):
            q = t * 128 + r
            ks = np.arange(kbase, kbase + 256)
            sbm[r, j * 256:(j + 1) * 256] = np.where(ks <= q, 0.0, MASKVAL)
    ptm = np.zeros((128, 16 * 256), np.float32)
    for cp in range(4):
        for li in range(4):
            tk = 4 * cp + li
            for c in range(256):
                sl = 2 * cp + c // 128
                q = tiles[sl] * 128 + (c % 128)
                kg = tk * 128 + np.arange(128)
                ptm[:, (cp * 4 + li) * 256 + c] = (kg <= q).astype(np.float32)
    iota = np.tile(np.arange(8, dtype=np.float32), (128, 1))
    return qtrue, selc, sbm, ptm, iota


def _rope_n(tvals):
    inv = (10000.0 ** (-np.arange(64) / 64.0))
    th = np.outer(inv, tvals)
    return np.rint(th / (2 * np.pi)).astype(np.float32)


def kernel(**inputs):
    global _PROG
    x = np.ascontiguousarray(np.asarray(inputs["L0_attn_hidden_states"],
                                        np.float32))
    Wq = np.asarray(inputs["L0_attn_Wq"], np.float32)
    Wk = np.asarray(inputs["L0_attn_Wk"], np.float32)
    Wv = np.asarray(inputs["L0_attn_Wv"], np.float32)
    Wo = np.asarray(inputs["L0_attn_Wo"], np.float32)
    W = np.asarray(inputs["L0_ffn_W"], np.float32)
    fb = np.asarray(inputs["L0_ffn_bias"], np.float32)
    A = np.asarray(inputs["L0_ffn_A"], np.float32)
    Bf = np.asarray(inputs["L0_ffn_Bf"], np.float32)
    co = np.asarray(inputs["L0_ffn_coeffs"], np.float32)
    sc = np.asarray(inputs["L0_ffn_scales"], np.float32)

    wqT = np.ascontiguousarray(Wq.T)
    wkT = np.ascontiguousarray(Wk.T)
    wvT = np.ascontiguousarray(Wv.T)
    woT = np.ascontiguousarray(Wo.T)
    wT = np.ascontiguousarray(W.T)
    a_st = np.ascontiguousarray(A.transpose(2, 0, 1).reshape(D, 256))
    bf_st = np.ascontiguousarray(Bf.transpose(0, 2, 1).reshape(256, D))

    tf = np.arange(S, dtype=np.float32)[None, :]
    nf = _rope_n(np.arange(S))
    ivc = np.arange(64, dtype=np.float32)[:, None]

    if _PROG is None:
        _PROG = build_program()

    in_maps = []
    for c in range(8):
        bb, p = c // 2, c % 2
        qtrue, selc, sbm, ptm, iota = _host_aux(p)
        xo = np.ascontiguousarray(x[bb, qtrue, :])
        tow = qtrue.astype(np.float32)[None, :]
        im = dict(
            x_full=np.ascontiguousarray(x[bb]), x_own=xo,
            wqT=wqT, wkT=wkT, wvT=wvT, woT=woT, wT=wT,
            a_st=a_st, bf_st=bf_st, fbias=fb[None, :],
            coeffs=co[:, None], scales=sc[:, None],
            t_full=tf, t_own=tow, n_full=nf, n_own=_rope_n(qtrue),
            ivec=ivc, selc=selc, sbmask=sbm, ptmask=ptm, iota8=iota,
        )
        in_maps.append(im)

    res = run_bass_kernel_spmd(_PROG, in_maps, core_ids=list(range(8)))
    out = np.zeros((B, S, D), np.float32)
    for c in range(8):
        bb, p = c // 2, c % 2
        qtrue, *_ = _host_aux(p)
        out[bb, qtrue, :] = res.results[c]["out_own"]
    return out


# version 08 (final)
# speedup vs baseline: 1.0282x; 1.0282x over previous
import math
import contextlib
import numpy as np
import ml_dtypes
import concourse.bass as bass
import concourse.mybir as mybir
from concourse.bass_utils import run_bass_kernel_spmd

F32 = mybir.dt.float32
BF16 = mybir.dt.bfloat16
U8 = mybir.dt.uint8
AF = mybir.ActivationFunctionType
OP = mybir.AluOpType
AX = mybir.AxisListType
NBF = ml_dtypes.bfloat16

B, S, D = 4, 2048, 4096
H, HKV, HD = 32, 8, 128
REP = H // HKV
NSLOT = 8
TGT = 205
NPROBE = 13
G = 4
NG = H // G
MASKVAL = -28672.0
ISQ = float(1.0 / np.float32(math.sqrt(HD)))
C1 = 6.28125
C2 = float(np.float32(2 * math.pi - C1))
C3 = float(np.float32(2 * math.pi - C1 - np.float32(2 * math.pi - C1)))
SIGCLAMP = 80.0

KT = [2 * j + 2 for j in range(NSLOT)]
LS = [128 * k for k in KT]
NQT = 8


class Builder:
    def __init__(self):
        self.nc = bass.Bass("TRN2", target_bir_lowering=False)
        self.tasks = []
        self.ctr = {}
        self.sem = {}
        self.es = contextlib.ExitStack()

    def new_sem(self, *names):
        for n in names:
            self.sem[n] = self.es.enter_context(self.nc.semaphore(n))
            self.ctr[n] = 0

    def sb(self, name, shape, dt):
        return self.es.enter_context(self.nc.sbuf_tensor(name, shape, dt))

    def ps(self, name, shape, dt=F32):
        return self.es.enter_context(self.nc.psum_tensor(name, shape, dt))

    def t(self, eng, fn, waits=(), inc=None, amt=1):
        tok = None
        waits = [w for w in waits if w is not None]
        if inc is not None:
            if inc in ('ld', 'gd', 'st', 'lq', 'la', 'lx'):
                waits.append((inc, self.ctr[inc]))
            self.ctr[inc] += amt
            tok = self.ctr[inc]
        self.tasks.append((eng, fn, waits, inc, amt, tok))
        return tok

    def dma(self, q, out, in_, waits=(), inc='ld'):
        return self.t(q, lambda e: e.dma_start(out=out, in_=in_), waits=waits,
                      inc=inc, amt=16)

    def now(self, *engs):
        return [(e, self.ctr[e]) for e in engs if self.ctr[e] > 0]

    def emit(self, block):
        def run(engname, eng):
            hiwater = {}
            own = {}
            for teng, fn, waits, inc, amt, tok in self.tasks:
                if teng != engname:
                    continue
                for (s, v) in waits:
                    if v <= 0 or hiwater.get(s, 0) >= v:
                        continue
                    eng.wait_ge(self.sem[s], v)
                    hiwater[s] = v
                instr = fn(eng)
                if inc is not None:
                    assert instr is not None, f"{engname} task returned None"
                    instr.then_inc(self.sem[inc], amt)
                    own[inc] = max(own.get(inc, 0), tok)

        @block.sync
        def _(e):
            run('sy', e)

        @block.vector
        def _(e):
            run('ve', e)

        @block.scalar
        def _(e):
            run('ac', e)

        @block.tensor
        def _(e):
            run('pe', e)

        @block.gpsimd
        def _(e):
            run('po', e)


def cdiv(a, b):
    return (a + b - 1) // b


def chunked_wT(ap):
    return ap.rearrange("(c p) n -> p c n", p=128)


def build_program():
    b = Builder()
    nc = b.nc
    E = b.sb

    def din(name, shape, dt=F32):
        return nc.dram_tensor(name, shape, dt, kind="ExternalInput").ap()

    x_full = din("x_full", [S, D])
    x_own = din("x_own", [1024, D])
    wqT = din("wqT", [D, D])
    wkT = din("wkT", [D, HKV * HD])
    wvT = din("wvT", [D, HKV * HD])
    woT = din("woT", [D, D])
    wT = din("wT", [D, D])
    a_st = din("a_st", [D, 256])
    bf_st = din("bf_st", [256, D])
    fbias = din("fbias", [1, D])
    coeffs = din("coeffs", [16, 1])
    scales = din("scales", [16, 1])
    t_full = din("t_full", [1, S])
    t_own = din("t_own", [1, 1024])
    n_full = din("n_full", [64, S])
    n_own = din("n_own", [64, 1024])
    ivec = din("ivec", [64, 1])
    selc = din("selc", [128, NSLOT * 8])
    sbmask = din("sbmask", [128, NSLOT * 256])
    ptmask = din("ptmask", [128, 16 * 256])
    iota8 = din("iota8", [128, 8])

    import os as _os
    _dbg = "ExternalOutput" if _os.environ.get("KDBG") else "Internal"
    dbg = nc.dram_tensor("dbg", [4, 128, 1024], F32, kind=_dbg).ap()
    out_own = nc.dram_tensor("out_own", [1024, D], F32, kind="ExternalOutput").ap()
    attn_hbm = nc.dram_tensor("attn_hbm", [H, HD, 1024], BF16, kind=_dbg).ap()
    qr_hbm = nc.dram_tensor("qr_hbm", [H, 2, HD, 512], BF16, kind=_dbg).ap()
    xno_hbm = nc.dram_tensor("xno_hbm", [128, 32 * 1024], BF16, kind=_dbg).ap()
    zdram = nc.dram_tensor("zdram", [1, 256], F32, kind="Internal").ap()
    wdram = nc.dram_tensor("wdram", [16, 1], F32, kind="Internal").ap()

    b.new_sem('ld', 'gd', 'st', 've', 'ac', 'pe', 'po', 'lq', 'la', 'lx')

    ones1r = b.sb("ones1r", [1, 128], BF16)
    idbf = b.sb("idbf", [128, 128], BF16)

    pA = contextlib.ExitStack()
    E = lambda n, sh, dt: pA.enter_context(nc.sbuf_tensor(n, sh, dt))
    krT = E("krT", [128, HKV * S], BF16)
    vkm = E("vkm", [128, HKV * 16 * HD], BF16)
    cosq = E("cosq", [128, 1024], BF16)
    sinq = E("sinq", [128, 1024], BF16)
    cosk = E("cosk", [128, S], BF16)
    sink = E("sink", [128, S], BF16)
    selcs = E("selcs", [128, NSLOT * 8], F32)
    sbmasks = E("sbmasks", [128, NSLOT * 256], BF16)
    ptmasks = E("ptmasks", [128, 16 * 256], BF16)
    iota8s = E("iota8s", [128, 8], F32)
    ones128 = E("ones128", [128, 1], BF16)
    ones4r = E("ones4r", [4, 128], BF16)
    ones1f = E("ones1f", [1, 128], F32)
    ones64f = E("ones64f", [1, 64], F32)
    taustate = E("taustate", [128, H * NSLOT], F32)

    ST = {n: E("st_" + n, [128, 2 * G], F32)
          for n in ('LO', 'HI', 'NLO', 'NHI', 'T', 'CNT', 'F', 'SC1', 'SC2',
                    'TAU', 'JC')}
    STU = {n: E("stu_" + n, [128, 2 * G], U8) for n in ('GE', 'NE', 'UA', 'UB')}
    w8 = E("w8", [128, 2 * G * 8], F32)
    cpk = E("cpk", [128, 2 * G * 4], BF16)
    rrows = E("rrows", [4, NSLOT * G * 128], BF16)
    zrow = E("zrow", [1, 256], F32)
    zinv = E("zinv", [1, 256], F32)

    SL = {}
    for j in range(NSLOT):
        base = j * 8
        SL[j] = {n: selcs[:, base + i: base + i + 1] for i, n in enumerate(
            ('TGT', 'AIM', 'T32', 'T64A', 'T64B', 'KEEP', 'LP', 'NQ'))}

    a0 = b.dma('sy', selcs[:], selc)
    a1 = b.dma('sy', iota8s[:], iota8)
    a2 = b.t('po', lambda e: e.dma_start(out=sbmasks[:], in_=sbmask),
             inc='gd', amt=16)
    a3 = b.t('po', lambda e: e.dma_start(out=ptmasks[:], in_=ptmask),
             inc='gd', amt=16)

    def consts(e):
        e.memset(ones128[:], 1.0)
        e.memset(ones4r[:], 1.0)
        e.memset(ones1r[:], 1.0)
        e.memset(ones1f[:], 1.0)
        e.memset(ones64f[:], 1.0)
        return e.engine_nop()
    c0 = b.t('ve', consts, inc='ve')

    def mkident(e):
        e.memset(idbf[:], 0.0)
        return e.affine_select(out=idbf[:], in_=idbf[:], compare_op=OP.not_equal,
                               fill=1.0, base=0, pattern=[[-1, 128]],
                               channel_multiplier=1)
    cid = b.t('po', mkident, inc='po')


    p1 = contextlib.ExitStack()
    E1 = lambda n, s, d: p1.enter_context(nc.sbuf_tensor(n, s, d))
    ivs = E1("ivs", [64, 1], F32)
    k1e4 = E1("k1e4", [1, 1], F32)
    lnv = E1("lnv", [1, 1], F32)
    s64 = E1("s64", [64, 1], F32)
    invf = E1("invf", [64, 1], F32)
    trow = E1("trow", [1, S], F32)
    nrow = E1("nrow", [64, S], F32)
    theta = E1("theta", [64, S], F32)
    rrd = E1("rrd", [64, S], F32)
    ps_th = p1.enter_context(nc.psum_tensor("ps_th", [64, 512], F32))

    t1 = b.dma('sy', ivs[:], ivec)
    t1b = b.t('ve', lambda e: e.memset(s64[:], 10000.0), waits=[('ld', t1)],
              inc='ve')
    t2 = b.t('ac', lambda e: e.activation(out=s64[:], in_=s64[:], func=AF.Ln),
             waits=[('ve', t1b)], inc='ac')
    t4 = b.t('ve', lambda e: e.tensor_scalar(out=s64[:], in0=s64[:],
                                             scalar1=-1.0 / 64, scalar2=None,
                                             op0=OP.mult),
             waits=[('ac', t2)], inc='ve')
    t4b = b.t('ve', lambda e: e.tensor_tensor(out=s64[:], in0=s64[:], in1=ivs[:],
                                              op=OP.mult), waits=[('ve', t4)],
              inc='ve')
    t5 = b.t('ac', lambda e: e.activation(out=invf[:], in_=s64[:], func=AF.Exp),
             waits=[('ve', t4b)], inc='ac')

    def build_tables(tdram, ndram, n_t, cosd, sind):
        nl = b.dma('sy', nrow[:, :n_t], ndram, waits=b.now('ve'))
        trep = b.t('sy', lambda e: e.dma_start(
            out=theta[:, :n_t],
            in_=tdram.rearrange("p (o n) -> p o n", o=1).to_broadcast([1, 64, n_t])),
            waits=b.now('pe', 've', 'ac'), inc='ld', amt=16)
        last = b.t('ve', lambda e: e.tensor_tensor(
            out=theta[:, :n_t], in0=theta[:, :n_t],
            in1=invf[:, 0:1].to_broadcast([64, n_t]), op=OP.mult),
            waits=[('ld', trep), ('ac', t5)], inc='ve')

        def cw(e):
            e.scalar_tensor_tensor(out=rrd[:, :n_t], in0=nrow[:, :n_t],
                                   scalar=-C1, in1=theta[:, :n_t],
                                   op0=OP.mult, op1=OP.add)
            e.scalar_tensor_tensor(out=rrd[:, :n_t], in0=nrow[:, :n_t],
                                   scalar=-C2, in1=rrd[:, :n_t],
                                   op0=OP.mult, op1=OP.add)
            return e.scalar_tensor_tensor(out=rrd[:, :n_t], in0=nrow[:, :n_t],
                                          scalar=-C3, in1=rrd[:, :n_t],
                                          op0=OP.mult, op1=OP.add)
        tcw = b.t('ve', cw, waits=[('ld', nl), ('ve', last)], inc='ve')
        ts1 = b.t('ac', lambda e: e.activation(out=sind[0:64, :n_t],
                                               in_=rrd[:, :n_t], func=AF.Sin),
                  waits=[('ve', tcw)], inc='ac')
        ts2 = b.t('ac', lambda e: e.activation(out=theta[:, :n_t],
                                               in_=rrd[:, :n_t], func=AF.Sin,
                                               scale=0.5),
                  inc='ac')
        tc1 = b.t('ve', lambda e: e.scalar_tensor_tensor(
            out=theta[:, :n_t], in0=theta[:, :n_t], scalar=-2.0,
            in1=theta[:, :n_t], op0=OP.mult, op1=OP.mult),
            waits=[('ac', ts2)], inc='ve')
        tc2 = b.t('ve', lambda e: e.tensor_scalar(
            out=cosd[0:64, :n_t], in0=theta[:, :n_t], scalar1=1.0, scalar2=None,
            op0=OP.add), inc='ve')
        d1 = b.t('ve', lambda e: e.tensor_scalar(
            out=cosd[64:128, :n_t], in0=cosd[0:64, :n_t], scalar1=1.0,
            scalar2=None, op0=OP.mult), inc='ve')
        d2 = b.t('ve', lambda e: e.tensor_scalar(
            out=sind[64:128, :n_t], in0=sind[0:64, :n_t], scalar1=1.0,
            scalar2=None, op0=OP.mult), waits=[('ac', ts1)], inc='ve')
        return [('ac', ts1), ('ve', d1), ('ve', d2)]

    rk = build_tables(t_full, n_full, S, cosk, sink)
    rq = build_tables(t_own, n_own, 1024, cosq, sinq)

    p1.close()
    p2 = contextlib.ExitStack()
    E2 = lambda n, s, d: p2.enter_context(nc.sbuf_tensor(n, s, d))
    xtile = [E2("xtile0", [128, D], F32)]
    xnb = [E2("xnb0", [128, D], BF16)]
    lnmu = E2("lnmu", [128, 1], F32)
    lnnv = E2("lnnv", [128, 1], F32)
    lnrr = E2("lnrr", [128, 1], F32)
    rsc = E2("rsc", [128, 1024], BF16)
    xnT_blk = E2("xnT_blk", [128, 32 * 512], BF16)
    wbuf = [E2(f"wbuf{i}", [128, 2 * 32 * 128], BF16) for i in range(2)]
    kvtmp = E2("kvtmp", [128, 512], F32)
    kvb = E2("kvb", [128, 512], BF16)
    ps_tr = [p2.enter_context(nc.psum_tensor(f"ps_tr{i}", [128, 128], BF16))
             for i in range(2)]
    ps_mm = [p2.enter_context(nc.psum_tensor(f"ps_mm{i}", [128, 512], F32))
             for i in range(2)]

    def ln_tile(xt, dst, waits, mu, nv, rr):
        t1 = b.t('ve', lambda e: e.tensor_scalar(
            out=dst[:], in0=xt[:], scalar1=1.0, scalar2=None, op0=OP.mult,
            op1=OP.add, accum_out=mu[:]), waits=waits, inc='ve')
        t2 = b.t('ve', lambda e: e.tensor_scalar(
            out=mu[:], in0=mu[:], scalar1=-1.0 / D, scalar2=None,
            op0=OP.mult), waits=[('ve', t1)], inc='ve')
        t3 = b.t('ac', lambda e: e.activation(
            out=dst[:], in_=xt[:], func=AF.Square, bias=mu[:],
            accum_out=nv[:]), waits=[('ve', t2)], inc='ac')
        t4 = b.t('ve', lambda e: e.tensor_scalar(
            out=nv[:], in0=nv[:], scalar1=1.0 / D, scalar2=1e-5,
            op0=OP.mult, op1=OP.add), waits=[('ac', t3)], inc='ve')
        t5 = b.t('ac', lambda e: e.activation(out=nv[:], in_=nv[:],
                                              func=AF.Sqrt),
                 waits=[('ve', t4)], inc='ac')
        t6 = b.t('ve', lambda e: e.reciprocal(out=rr[:], in_=nv[:]),
                 waits=[('ac', t5)], inc='ve')
        return b.t('ve', lambda e: e.tensor_scalar(
            out=dst[:], in0=xt[:], scalar1=mu[:], scalar2=rr[:],
            op0=OP.add, op1=OP.mult), waits=[('ve', t6)], inc='ve')

    def transpose_to(src_bf, dst_ap_fn, n_chunks, extra_waits, trp):
        toks = []
        for c in range(n_chunks):
            pb = c % 2
            tk = b.t('pe', lambda e, c=c, pb=pb: e.transpose(
                out=trp[pb][:], in_=src_bf[:, c * 128:(c + 1) * 128],
                identity=idbf[:]),
                waits=extra_waits + [('po', cid)] + b.now('ve'), inc='pe')
            toks.append(b.t('ve', lambda e, c=c, pb=pb: e.tensor_copy(
                out=dst_ap_fn(c), in_=trp[pb][:]), waits=[('pe', tk)],
                inc='ve'))
        return toks

    for blk in range(4):
        for tt in range(4):
            buf = 0
            tok = blk * 512 + tt * 128
            tl = b.dma('sy', xtile[buf][:], x_full[tok:tok + 128, :],
                       waits=b.now('ve', 'po') + (rk + rq if (blk == 0 and tt == 0) else []))
            tln = ln_tile(xtile[buf], xnb[buf], [('ld', tl)], lnmu, lnnv, lnrr)
            transpose_to(
                xnb[buf],
                lambda c, tt=tt: xnT_blk[:, c * 512 + tt * 128:
                                         c * 512 + tt * 128 + 128],
                32, [('ve', tln)], ps_tr)
        xnT_tok = b.ctr['ve']
        for g in range(HKV):
            wb = g % 2
            wkl = b.t('po', lambda e, g=g, wb=wb: e.dma_start(
                out=wbuf[wb][:, 0:4096].rearrange("p (c n) -> p c n", n=128),
                in_=chunked_wT(wkT)[:, :, g * 128:(g + 1) * 128]),
                waits=b.now('pe'), inc='gd', amt=16)
            wvl = b.t('po', lambda e, g=g, wb=wb: e.dma_start(
                out=wbuf[wb][:, 4096:8192].rearrange("p (c n) -> p c n", n=128),
                in_=chunked_wT(wvT)[:, :, g * 128:(g + 1) * 128]),
                inc='gd', amt=16)
            for c in range(32):
                tk = b.t('pe', lambda e, c=c, wb=wb: e.matmul(
                    out=ps_mm[0][:], lhsT=wbuf[wb][:, c * 128:(c + 1) * 128],
                    rhs=xnT_blk[:, c * 512:(c + 1) * 512],
                    start=(c == 0), stop=(c == 31)),
                    waits=([('gd', wkl), ('ve', xnT_tok)] + b.now('ac'))
                    if c == 0 else [], inc='pe' if c == 31 else None)
            kev = b.t('ac', lambda e: e.copy(out=kvtmp[:], in_=ps_mm[0][:]),
                      waits=[('pe', tk)] + b.now('ve', 'po'), inc='ac')
            sl = slice(blk * 512, blk * 512 + 512)
            dst = krT[:, g * S + blk * 512: g * S + blk * 512 + 512]

            def ropek(e, dst=dst, sl=sl):
                e.tensor_copy(out=kvb[:], in_=kvtmp[:])
                e.tensor_scalar(out=rsc[0:64, 0:512], in0=kvb[64:128, :],
                                scalar1=-1.0, scalar2=None, op0=OP.mult)
                e.tensor_scalar(out=rsc[64:128, 0:512], in0=kvb[0:64, :],
                                scalar1=1.0, scalar2=None, op0=OP.mult)
                e.tensor_tensor(out=dst[:, :], in0=kvb[:], in1=cosk[:, sl],
                                op=OP.mult)
                e.tensor_tensor(out=rsc[:, 0:512], in0=rsc[:, 0:512],
                                in1=sink[:, sl], op=OP.mult)
                return e.tensor_add(out=dst[:, :], in0=dst[:, :],
                                    in1=rsc[:, 0:512])
            b.t('ve', ropek, waits=[('ac', kev)] +
                (rk + rq if (blk == 0 and g == 0) else []), inc='ve')
            for c in range(32):
                tv = b.t('pe', lambda e, c=c, wb=wb: e.matmul(
                    out=ps_mm[1][:],
                    lhsT=wbuf[wb][:, 4096 + c * 128:4096 + (c + 1) * 128],
                    rhs=xnT_blk[:, c * 512:(c + 1) * 512],
                    start=(c == 0), stop=(c == 31)),
                    waits=([('gd', wvl)] + b.now('ve', 'po')) if c == 0 else [],
                    inc='pe' if c == 31 else None)
            vev = b.t('ve', lambda e: e.tensor_copy(out=kvb[:], in_=ps_mm[1][:]),
                      waits=[('pe', tv)] + b.now('po'), inc='ve')
            for q4 in range(4):
                ktile = blk * 4 + q4
                tvt = b.t('pe', lambda e, q4=q4: e.transpose(
                    out=ps_tr[q4 % 2][:], in_=kvb[:, q4 * 128:(q4 + 1) * 128],
                    identity=idbf[:]), waits=[('ve', vev)] + b.now('ve'),
                    inc='pe')
                b.t('ve', lambda e, g=g, ktile=ktile, q4=q4: e.tensor_copy(
                    out=vkm[:, (g * 16 + ktile) * 128:(g * 16 + ktile) * 128 + 128],
                    in_=ps_tr[q4 % 2][:]), waits=[('pe', tvt)], inc='ve')
    kv_ve = b.ctr['ve']

    p2.close()
    p3 = contextlib.ExitStack()
    E3 = lambda n, s, d: p3.enter_context(nc.sbuf_tensor(n, s, d))
    xtileQ = [E3("xtileQ", [128, D], F32)]
    xnbQ = [E3("xnbQ", [128, D], BF16)]
    lnmuQ = E3("lnmuQ", [128, 1], F32)
    lnnvQ = E3("lnnvQ", [128, 1], F32)
    lnrrQ = E3("lnrrQ", [128, 1], F32)
    rscQ = E3("rscQ", [128, 512], BF16)
    wbufQ = [E3("wbufQ0", [128, 32 * 128], BF16)]
    xnoT = E3("xnoT", [128, 32 * 1024], BF16)
    qtmp = E3("qtmp", [128, 512], F32)
    qrb = E3("qrb", [128, 512], BF16)
    qrr = E3("qrr", [128, 512], BF16)
    ps_trQ = [p3.enter_context(nc.psum_tensor(f"ps_trQ{i}", [128, 128], BF16))
             for i in range(2)]
    ps_mmQ = [p3.enter_context(nc.psum_tensor(f"ps_mmQ{i}", [128, 512], F32))
             for i in range(2)]
    _lt = [lnmuQ, lnnvQ, lnrrQ]

    for tt in range(NQT):
        buf = 0
        tl = b.dma('sy', xtileQ[buf][:], x_own[tt * 128:tt * 128 + 128, :],
                   waits=b.now('ve', 'po', 'ac'))
        tln = ln_tile(xtileQ[buf], xnbQ[buf], [('ld', tl)], lnmuQ, lnnvQ, lnrrQ)
        transpose_to(
            xnbQ[buf],
            lambda c, tt=tt: xnoT[:, c * 1024 + tt * 128:c * 1024 + tt * 128 + 128],
            32, [('ve', tln)], ps_trQ)
    xnoT_tok = b.ctr['ve']
    b.dma('sy', xno_hbm, xnoT[:], waits=[('ve', xnoT_tok)], inc='st')

    for h in range(H):
        wb = 0
        wql = b.t('po', lambda e, h=h, wb=wb: e.dma_start(
            out=wbufQ[wb][:, 0:32 * 128].rearrange("p (c n) -> p c n", n=128),
            in_=chunked_wT(wqT)[:, :, h * 128:(h + 1) * 128]),
            waits=b.now('pe'), inc='gd', amt=16)
        for qb in range(2):
            for c in range(32):
                tk = b.t('pe', lambda e, c=c, qb=qb, wb=wb: e.matmul(
                    out=ps_mmQ[qb % 2][:], lhsT=wbufQ[wb][:, c * 128:(c + 1) * 128],
                    rhs=xnoT[:, c * 1024 + qb * 512:c * 1024 + qb * 512 + 512],
                    start=(c == 0), stop=(c == 31)),
                    waits=([('gd', wql), ('ve', xnoT_tok)] + b.now('ac', 've'))
                    if c == 0 else [], inc='pe' if c == 31 else None)
            qev = b.t('ac', lambda e, qb=qb: e.copy(out=qtmp[:],
                                                    in_=ps_mmQ[qb % 2][:]),
                      waits=[('pe', tk)] + b.now('ve', 'po'), inc='ac')
            sl = slice(qb * 512, qb * 512 + 512)

            def ropeq(e, sl=sl):
                e.tensor_copy(out=qrb[:], in_=qtmp[:])
                e.tensor_scalar(out=rscQ[0:64, 0:512], in0=qrb[64:128, :],
                                scalar1=-1.0, scalar2=None, op0=OP.mult)
                e.tensor_scalar(out=rscQ[64:128, 0:512], in0=qrb[0:64, :],
                                scalar1=1.0, scalar2=None, op0=OP.mult)
                e.tensor_tensor(out=qrr[:, :], in0=qrb[:], in1=cosq[:, sl],
                                op=OP.mult)
                e.tensor_tensor(out=rscQ[:, 0:512], in0=rscQ[:, 0:512],
                                in1=sinq[:, sl], op=OP.mult)
                return e.tensor_add(out=qrr[:, :], in0=qrr[:, :],
                                    in1=rscQ[:, 0:512])
            trq = b.t('ve', ropeq, waits=[('ac', qev), ('st', b.ctr['st'])],
                      inc='ve')
            b.t('sy', lambda e, h=h, qb=qb: e.dma_start(
                out=qr_hbm[h, qb], in_=qrr[:]),
                waits=[('ve', trq)], inc='st', amt=16)

    q_st = b.ctr['st']

    p3.close()

    p4 = contextlib.ExitStack()
    E4 = lambda n, s, d: p4.enter_context(nc.sbuf_tensor(n, s, d))
    qr_hg = E4("qr_hg", [128, G * 1024], BF16)
    sbA = E4("sbA", [128, G * LS[3]], BF16)
    sbB = E4("sbB", [128, G * LS[7]], BF16)
    scrV = E4("scrV", [128, 2048], BF16)
    scrP = E4("scrP", [128, 2048], BF16)
    scrA = E4("scrA", [128, 2048], BF16)
    ubuf = E4("ubuf", [128, 2048], F32)
    u2buf = E4("u2buf", [128, 2048], F32)
    ebuf = E4("ebuf", [128, 256], BF16)
    ptbuf = E4("ptbuf", [128, 256], BF16)
    attb = E4("attb", [128, 256], BF16)
    zbf = E4("zbf", [128, 256], F32)
    ps_sq = [p4.enter_context(nc.psum_tensor(f"ps_sq{i}", [128, 512], F32))
             for i in range(2)]
    ps_st = [p4.enter_context(nc.psum_tensor(f"ps_st{i}", [128, 256], F32))
             for i in range(2)]
    ps_av = p4.enter_context(nc.psum_tensor("ps_av", [128, 256], F32))
    ps_z = p4.enter_context(nc.psum_tensor("ps_z", [1, 256], F32))
    ps_zb = p4.enter_context(nc.psum_tensor("ps_zb", [128, 256], F32))
    ps_t4 = p4.enter_context(nc.psum_tensor("ps_t4", [4, 128], BF16))

    def sb_ap(sp, hh, j):
        slab = sbA if sp == 0 else sbB
        return slab[:, hh * (LS[3] if sp == 0 else LS[7]):
                    hh * (LS[3] if sp == 0 else LS[7]) + LS[j]]

    sb_war = []
    sq_war = {0: None, 1: None}
    for hg in range(NG):
        h0 = hg * G
        qld = []
        for hh in range(G):
            for qb in range(2):
                t_ = b.dma('sy',
                           qr_hg[:, hh * 1024 + qb * 512: hh * 1024 + qb * 512 + 512],
                           qr_hbm[h0 + hh, qb],
                           waits=[('st', q_st)] + b.now('pe'), inc='lq')
            qld.append(t_)

        for batch in range(4):
            slots = (batch, batch + 4)
            for sp, j in enumerate(slots):
                for hh in range(G):
                    g = (h0 + hh) // REP
                    nkb = cdiv(LS[j], 512)
                    for kb in range(nkb):
                        kw = min(512, LS[j] - kb * 512)
                        pb = (kb + sp * 2) % 2
                        tk = b.t('pe', lambda e, hh=hh, j=j, kb=kb, kw=kw, pb=pb, g=g: e.matmul(
                            out=ps_sq[pb][:, 0:kw],
                            lhsT=qr_hg[:, hh * 1024 + j * 128: hh * 1024 + j * 128 + 128],
                            rhs=krT[:, g * S + kb * 512: g * S + kb * 512 + kw],
                            start=True, stop=True),
                            waits=[('lq', qld[hh]), ('ve', kv_ve)] + sb_war +
                            ([sq_war[pb]] if sq_war[pb] else []), inc='pe')
                        ev = b.t('ac', lambda e, sp=sp, hh=hh, j=j, kb=kb, kw=kw, pb=pb: e.copy(
                            out=sb_ap(sp, hh, j)[:, kb * 512: kb * 512 + kw],
                            in_=ps_sq[pb][:, 0:kw]),
                            waits=[('pe', tk), ('gd', a2)], inc='ac')
                        sq_war[pb] = ('ac', ev)
                    b.t('ve', lambda e, sp=sp, hh=hh, j=j: e.tensor_tensor(
                        out=sb_ap(sp, hh, j)[:, LS[j] - 256: LS[j]],
                        in0=sb_ap(sp, hh, j)[:, LS[j] - 256: LS[j]],
                        in1=sbmasks[:, j * 256:(j + 1) * 256],
                        op=OP.add), waits=b.now('ac'), inc='ve')
                    col = sp * G + hh
                    def sig(e, sp=sp, hh=hh, j=j, col=col):
                        return e.scalar_tensor_tensor(
                            out=scrV[:, 0:128], in0=sb_ap(sp, hh, j)[:, 0:128],
                            scalar=1.0, in1=sb_ap(sp, hh, j)[:, 0:128],
                            op0=OP.mult, op1=OP.mult,
                            accum_out=ST['CNT'][:, col:col + 1])
                    b.t('ve', sig, inc='ve')
            sb_ac = b.ctr['ac']
            sb_ve = b.ctr['ve']

            b.t('ve', lambda e: e.tensor_scalar(
                out=ST['CNT'][:], in0=ST['CNT'][:], scalar1=1.0 / 128,
                scalar2=None, op0=OP.mult), waits=[('ve', sb_ve), ('ld', a0)],
                inc='ve')
            tqq = b.t('ac', lambda e: e.activation(out=ST['F'][:],
                                                   in_=ST['CNT'][:],
                                                   func=AF.Sqrt),
                      waits=b.now('ve'), inc='ac')

            ti1 = b.t('ve', lambda e: e.tensor_scalar(
                out=ST['F'][:], in0=ST['F'][:], scalar1=SIGCLAMP,
                scalar2=None, op0=OP.min), waits=[('ac', tqq)], inc='ve')

            def init2(e, slots=slots):
                e.tensor_scalar(out=ST['LO'][:], in0=ST['F'][:], scalar1=-8.0,
                                scalar2=-1.0, op0=OP.mult, op1=OP.add)
                e.tensor_scalar(out=ST['HI'][:], in0=ST['F'][:], scalar1=8.0,
                                scalar2=1.0, op0=OP.mult, op1=OP.add)
                for sp, j in enumerate(slots):
                    sl = slice(sp * G, sp * G + G)
                    e.tensor_scalar(out=ST['NLO'][:, sl],
                                    in0=SL[j]['NQ'].to_broadcast([128, G]),
                                    scalar1=1.0, scalar2=None, op0=OP.mult)
                return e.memset(ST['NHI'][:], 0.0)
            ti2 = b.t('ve', init2, waits=[('ve', ti1)], inc='ve')

            def init3(e, slots=slots, hg=hg, h0=h0):
                for sp, j in enumerate(slots):
                    for hh in range(G):
                        h = h0 + hh
                        col = sp * G + hh
                        dst = ST['T'][:, col:col + 1]
                        if j in (0, 4):
                            if hg == 0:
                                e.tensor_scalar(out=dst, in0=ST['HI'][:, col:col + 1],
                                                scalar1=0.15, scalar2=None,
                                                op0=OP.mult)
                            else:
                                e.tensor_copy(out=dst,
                                              in_=taustate[:, (h - G) * 8 + j:
                                                           (h - G) * 8 + j + 1])
                        else:
                            e.tensor_copy(out=dst,
                                          in_=taustate[:, h * 8 + j - 1:
                                                       h * 8 + j])
                return e.tensor_scalar(out=ST['SC1'][:], in0=ST['HI'][:],
                                       scalar1=-1e-3, scalar2=None, op0=OP.add)
            ti3 = b.t('ve', init3, waits=[('ve', ti2)], inc='ve')

            def init4(e):
                e.tensor_tensor(out=ST['T'][:], in0=ST['T'][:], in1=ST['SC1'][:],
                                op=OP.min)
                return e.tensor_scalar(out=ST['SC2'][:], in0=ST['LO'][:],
                                       scalar1=1e-3, scalar2=None, op0=OP.add)
            ti4 = b.t('ve', init4, waits=[('ve', ti3)], inc='ve')
            tprev = b.t('ve', lambda e: e.tensor_tensor(
                out=ST['T'][:], in0=ST['T'][:], in1=ST['SC2'][:], op=OP.max),
                waits=[('ve', ti4)], inc='ve')

            for it in range(NPROBE):
                cw = []
                for sp, j in enumerate(slots):
                    for hh in range(G):
                        col = sp * G + hh
                        lane = (col + it) % 8
                        eng = 've' if lane < 6 else 'ac'
                        if eng != 'ac':
                            scr = scrV
                            tk = b.t(eng, lambda e, sp=sp, hh=hh, j=j, col=col, scr=scr: e.tensor_scalar(
                                out=scr[:, 0:LS[j]], in0=sb_ap(sp, hh, j)[:],
                                scalar1=ST['T'][:, col:col + 1], scalar2=None,
                                op0=OP.is_gt, op1=OP.add,
                                accum_out=ST['CNT'][:, col:col + 1]),
                                waits=[('ve', tprev), ('ac', sb_ac)], inc=eng)
                            cw.append((eng, tk))
                        else:
                            tn = b.t('ve', lambda e, col=col: e.tensor_scalar(
                                out=ST['SC2'][:, col:col + 1],
                                in0=ST['T'][:, col:col + 1], scalar1=-1.0,
                                scalar2=None, op0=OP.mult),
                                waits=[('ve', tprev)], inc='ve')
                            ta = b.t('ac', lambda e, sp=sp, hh=hh, j=j, col=col: e.activation(
                                out=scrA[:, 0:LS[j]], in_=sb_ap(sp, hh, j)[:],
                                func=AF.Sign, bias=ST['SC2'][:, col:col + 1],
                                accum_out=ST['CNT'][:, col:col + 1]),
                                waits=[('ve', tn), ('ac', sb_ac)], inc='ac')
                            tb = b.t('ve', lambda e, col=col: e.tensor_scalar(
                                out=ST['CNT'][:, col:col + 1],
                                in0=ST['CNT'][:, col:col + 1], scalar1=0.5,
                                scalar2=None, op0=OP.mult),
                                waits=[('ac', ta)], inc='ve')
                            tb = b.t('ve', lambda e, col=col, j=j: e.scalar_tensor_tensor(
                                out=ST['CNT'][:, col:col + 1], in0=SL[j]['LP'],
                                scalar=0.5, in1=ST['CNT'][:, col:col + 1],
                                op0=OP.mult, op1=OP.add),
                                waits=[('ve', tb)], inc='ve')
                            cw.append(('ve', tb))

                def bk1(e, slots=slots):
                    for sp, j in enumerate(slots):
                        sl = slice(sp * G, sp * G + G)
                        e.tensor_scalar(out=STU['GE'][:, sl], in0=ST['CNT'][:, sl],
                                        scalar1=SL[j]['TGT'], scalar2=None,
                                        op0=OP.is_ge)
                        e.tensor_scalar(out=STU['NE'][:, sl], in0=ST['CNT'][:, sl],
                                        scalar1=SL[j]['TGT'], scalar2=None,
                                        op0=OP.is_lt)
                    return e.engine_nop()
                tbk1 = b.t('ve', bk1, waits=cw, inc='ve')

                def bk2(e):
                    e.copy_predicated(out=ST['LO'][:], mask=STU['GE'][:],
                                      data=ST['T'][:])
                    e.copy_predicated(out=ST['NLO'][:], mask=STU['GE'][:],
                                      data=ST['CNT'][:])
                    e.copy_predicated(out=ST['HI'][:], mask=STU['NE'][:],
                                      data=ST['T'][:])
                    return e.copy_predicated(out=ST['NHI'][:], mask=STU['NE'][:],
                                             data=ST['CNT'][:])
                tbk2 = b.t('ve', bk2, waits=[('ve', tbk1)], inc='ve')
                if it == NPROBE - 1:
                    tprev = tbk2
                    continue

                def bk3(e, slots=slots, it=it):
                    e.scalar_tensor_tensor(out=ST['SC2'][:], in0=ST['NHI'][:],
                                           scalar=-1.0, in1=ST['NLO'][:],
                                           op0=OP.mult, op1=OP.add)
                    for sp, j in enumerate(slots):
                        sl = slice(sp * G, sp * G + G)
                        e.tensor_scalar(out=ST['SC1'][:, sl],
                                        in0=ST['NLO'][:, sl],
                                        scalar1=SL[j]['AIM'], scalar2=None,
                                        op0=OP.subtract)
                    for sp, j in enumerate(slots):
                        sl = slice(sp * G, sp * G + G)
                        e.tensor_scalar(out=STU['UB'][:, sl],
                                        in0=ST['NHI'][:, sl],
                                        scalar1=SL[j]['T32'], scalar2=None,
                                        op0=OP.is_ge)
                    e.tensor_tensor(out=ST['JC'][:], in0=ST['HI'][:],
                                    in1=ST['LO'][:], op=OP.subtract)
                    if it % 2 == 1:
                        for sp, j in enumerate(slots):
                            sl = slice(sp * G, sp * G + G)
                            e.tensor_scalar(out=STU['UA'][:, sl],
                                            in0=ST['NHI'][:, sl],
                                            scalar1=SL[j]['T64A'],
                                            scalar2=None, op0=OP.is_lt)
                            e.tensor_scalar(out=STU['GE'][:, sl],
                                            in0=ST['NLO'][:, sl],
                                            scalar1=SL[j]['T64B'],
                                            scalar2=None, op0=OP.is_gt)
                        e.memset(ST['TAU'][:], 0.5)
                    return e.tensor_scalar(out=ST['SC2'][:], in0=ST['SC2'][:],
                                           scalar1=1.0, scalar2=None, op0=OP.max)
                tbk3 = b.t('ve', bk3, waits=[('ve', tbk2)], inc='ve')

                def bk4(e, it=it):
                    r = e.reciprocal(out=ST['SC2'][:], in_=ST['SC2'][:])
                    if it % 2 == 1:
                        r = e.tensor_tensor(out=STU['UA'][:], in0=STU['UA'][:],
                                            in1=STU['GE'][:], op=OP.bitwise_or)
                    return r
                tbk4 = b.t('ve', bk4, waits=[('ve', tbk3)], inc='ve')
                tbk5 = b.t('ve', lambda e: e.tensor_tensor(
                    out=ST['F'][:], in0=ST['SC1'][:], in1=ST['SC2'][:],
                    op=OP.mult), waits=[('ve', tbk4)], inc='ve')
                tbk6 = b.t('ve', lambda e: e.tensor_scalar(
                    out=ST['SC1'][:], in0=ST['F'][:], scalar1=0.5, scalar2=0.25,
                    op0=OP.mult, op1=OP.add), waits=[('ve', tbk5)], inc='ve')
                tbk7 = b.t('ve', lambda e: e.copy_predicated(
                    out=ST['SC1'][:], mask=STU['UB'][:], data=ST['F'][:]),
                    waits=[('ve', tbk6)], inc='ve')
                if it % 2 == 1:
                    tbk7 = b.t('ve', lambda e: e.copy_predicated(
                        out=ST['SC1'][:], mask=STU['UA'][:], data=ST['TAU'][:]),
                        waits=[('ve', tbk7)], inc='ve')
                tbk8 = b.t('ve', lambda e: e.tensor_scalar(
                    out=ST['SC1'][:], in0=ST['SC1'][:], scalar1=0.06,
                    scalar2=0.94, op0=OP.max, op1=OP.min),
                    waits=[('ve', tbk7)], inc='ve')
                tbk9 = b.t('ve', lambda e: e.tensor_tensor(
                    out=ST['SC2'][:], in0=ST['JC'][:], in1=ST['SC1'][:],
                    op=OP.mult), waits=[('ve', tbk8)], inc='ve')
                tprev = b.t('ve', lambda e: e.tensor_add(
                    out=ST['T'][:], in0=ST['LO'][:], in1=ST['SC2'][:]),
                    waits=[('ve', tbk9)], inc='ve')

            fw = []
            prev_tu2 = None
            for sp, j in enumerate(slots):
                for hh in range(G):
                    col = sp * G + hh
                    tu = b.t('ve', lambda e, sp=sp, hh=hh, j=j, col=col: e.tensor_scalar(
                        out=ubuf[:, 0:LS[j]], in0=sb_ap(sp, hh, j)[:],
                        scalar1=ST['LO'][:, col:col + 1], scalar2=0.0,
                        op0=OP.subtract, op1=OP.max),
                        waits=[('ve', tprev)] + ([('ve', prev_tu2)] if prev_tu2 else []),
                        inc='ve')
                    twd = b.t('ve', lambda e, col=col: e.tensor_tensor(
                        out=ST['SC1'][:, col:col + 1],
                        in0=ST['HI'][:, col:col + 1],
                        in1=ST['LO'][:, col:col + 1], op=OP.subtract), inc='ve')
                    tu2 = b.t('ve', lambda e, j=j, col=col: e.scalar_tensor_tensor(
                        out=u2buf[:, 0:LS[j]], in0=ubuf[:, 0:LS[j]],
                        scalar=ST['SC1'][:, col:col + 1], in1=ubuf[:, 0:LS[j]],
                        op0=OP.is_le, op1=OP.mult),
                        waits=[('ve', tu), ('ve', twd)], inc='ve')
                    t8 = b.t('ve', lambda e, col=col, j=j: e.max(
                        out=w8[:, col * 8:col * 8 + 8], in_=u2buf[:, 0:LS[j]]),
                        waits=[('ve', tu2)], inc='ve')
                    prev_tu2 = tu2
                    fw.append(('ve', t8))

            def fn1(e, slots=slots):
                for sp, j in enumerate(slots):
                    sl = slice(sp * G, sp * G + G)
                    e.tensor_scalar(out=ST['F'][:, sl], in0=ST['NHI'][:, sl],
                                    scalar1=SL[j]['TGT'], scalar2=-1.0,
                                    op0=OP.subtract, op1=OP.mult)
                return e.engine_nop()
            tf1 = b.t('ve', fn1, waits=fw, inc='ve')

            def fn2(e):
                e.tensor_scalar(out=ST['JC'][:], in0=ST['F'][:], scalar1=1.0,
                                scalar2=0.0, op0=OP.subtract, op1=OP.max)
                e.tensor_scalar(out=STU['UA'][:], in0=ST['F'][:], scalar1=1.0,
                                scalar2=None, op0=OP.is_lt)
                e.tensor_scalar(out=STU['UB'][:], in0=ST['F'][:], scalar1=7.0,
                                scalar2=None, op0=OP.is_gt)
                return e.tensor_scalar(out=ST['SC1'][:], in0=ST['LO'][:],
                                       scalar1=-1.0, scalar2=None, op0=OP.mult)
            tf2 = b.t('ve', fn2, waits=[('ve', tf1)], inc='ve')

            def fn3(e, slots=slots):
                e.tensor_scalar(out=ST['JC'][:], in0=ST['JC'][:], scalar1=6.0,
                                scalar2=None, op0=OP.min)
                e.tensor_tensor(out=STU['UA'][:], in0=STU['UA'][:],
                                in1=STU['UB'][:], op=OP.bitwise_or)
                e.tensor_tensor(out=ST['SC1'][:], in0=ST['SC1'][:],
                                in1=ST['LO'][:], op=OP.max)
                for sp, j in enumerate(slots):
                    sl = slice(sp * G, sp * G + G)
                    e.tensor_scalar(out=STU['NE'][:, sl], in0=ST['NLO'][:, sl],
                                    scalar1=SL[j]['TGT'], scalar2=None,
                                    op0=OP.is_equal)
                return e.memset(ST['SC2'][:], 1e-30)
            tf3 = b.t('ve', fn3, waits=[('ve', tf2)], inc='ve')

            def fn4(e):
                e.tensor_scalar(out=ST['F'][:], in0=ST['JC'][:], scalar1=1.0,
                                scalar2=None, op0=OP.add)
                for col in range(2 * G):
                    e.tensor_scalar(out=ubuf[:, col * 8: col * 8 + 8],
                                    in0=iota8s[:],
                                    scalar1=ST['JC'][:, col:col + 1],
                                    scalar2=None, op0=OP.is_equal)
                return e.tensor_scalar(out=ST['SC1'][:], in0=ST['SC1'][:],
                                       scalar1=-0.00390625, scalar2=-1e-6,
                                       op0=OP.mult, op1=OP.add)
            tf4 = b.t('ve', fn4, waits=[('ve', tf3)], inc='ve')

            def fn5(e):
                for col in range(2 * G):
                    e.tensor_scalar(out=u2buf[:, col * 8: col * 8 + 8],
                                    in0=iota8s[:],
                                    scalar1=ST['F'][:, col:col + 1],
                                    scalar2=None, op0=OP.is_equal)
                return e.copy_predicated(out=ST['SC1'][:], mask=STU['NE'][:],
                                         data=ST['SC2'][:])
            tf5 = b.t('ve', fn5, waits=[('ve', tf4)], inc='ve')

            tf6 = b.t('ve', lambda e: e.tensor_add(
                out=ubuf[:, 0:2 * G * 8], in0=ubuf[:, 0:2 * G * 8],
                in1=u2buf[:, 0:2 * G * 8]), waits=[('ve', tf5)], inc='ve')

            def fn7(e):
                for col in range(2 * G):
                    e.scalar_tensor_tensor(out=u2buf[:, col * 8: col * 8 + 8],
                                           in0=ubuf[:, col * 8: col * 8 + 8],
                                           scalar=0.5,
                                           in1=w8[:, col * 8:col * 8 + 8],
                                           op0=OP.mult, op1=OP.mult,
                                           accum_out=ST['TAU'][:, col:col + 1])
                return e.engine_nop()
            tf7 = b.t('ve', fn7, waits=[('ve', tf6)], inc='ve')

            tf8 = b.t('ve', lambda e: e.copy_predicated(
                out=ST['TAU'][:], mask=STU['UA'][:], data=ST['SC1'][:]),
                waits=[('ve', tf7)], inc='ve')

            def fn9(e, slots=slots):
                for sp, j in enumerate(slots):
                    sl = slice(sp * G, sp * G + G)
                    e.scalar_tensor_tensor(
                        out=ST['TAU'][:, sl],
                        in0=SL[j]['KEEP'].to_broadcast([128, G]), scalar=-1.0,
                        in1=ST['TAU'][:, sl], op0=OP.mult, op1=OP.add)
                return e.engine_nop()
            tf9 = b.t('ve', fn9, waits=[('ve', tf8)], inc='ve')

            def fn10(e, slots=slots, h0=h0):
                e.tensor_scalar(out=ST['SC1'][:], in0=ST['LO'][:], scalar1=-1.0,
                                scalar2=None, op0=OP.mult)
                e.tensor_scalar(out=ST['JC'][:], in0=ST['TAU'][:], scalar1=-1.0,
                                scalar2=None, op0=OP.mult)
                for sp, j in enumerate(slots):
                    for hh in range(G):
                        col = sp * G + hh
                        e.tensor_tensor(
                            out=taustate[:, (h0 + hh) * 8 + j:
                                         (h0 + hh) * 8 + j + 1],
                            in0=ST['LO'][:, col:col + 1],
                            in1=ST['TAU'][:, col:col + 1], op=OP.add)
                e.tensor_copy(out=cpk[:, 0:2 * G * 4:4], in_=ST['SC1'][:])
                return e.tensor_copy(out=cpk[:, 2:2 * G * 4:4], in_=ST['JC'][:])
            tf10 = b.t('ve', fn10, waits=[('ve', tf9)], inc='ve')

            def fn11(e):
                e.tensor_copy(out=ST['SC2'][:], in_=cpk[:, 0:2 * G * 4:4])
                return e.tensor_copy(out=ST['F'][:], in_=cpk[:, 2:2 * G * 4:4])
            tf11 = b.t('ve', fn11, waits=[('ve', tf10)], inc='ve')

            def fn12(e):
                e.tensor_tensor(out=ST['SC2'][:], in0=ST['SC1'][:],
                                in1=ST['SC2'][:], op=OP.subtract)
                return e.tensor_tensor(out=ST['F'][:], in0=ST['JC'][:],
                                       in1=ST['F'][:], op=OP.subtract)
            tf12 = b.t('ve', fn12, waits=[('ve', tf11)], inc='ve')

            def fn13(e):
                e.tensor_copy(out=cpk[:, 1:2 * G * 4:4], in_=ST['SC2'][:])
                return e.tensor_copy(out=cpk[:, 3:2 * G * 4:4], in_=ST['F'][:])
            tfin = b.t('ve', fn13, waits=[('ve', tf12)], inc='ve')

            for sp, j in enumerate(slots):
                for hh in range(G):
                    col = sp * G + hh
                    tk = b.t('pe', lambda e, col=col: e.transpose(
                        out=ps_t4[:], in_=cpk[:, col * 4:col * 4 + 4],
                        identity=idbf[:]),
                        waits=[('ve', tfin)] + b.now('ve'), inc='pe')
                    b.t('ve', lambda e, j=j, hh=hh: e.tensor_copy(
                        out=rrows[0:4, (j * G + hh) * 128:(j * G + hh) * 128 + 128],
                        in_=ps_t4[:]), waits=[('pe', tk)], inc='ve')
            sb_war = [('ve', b.ctr['ve']), ('ac', b.ctr['ac'])]

            if batch in (1, 3):
                pairs = (0, 2) if batch == 1 else (1, 3)
                for cp in pairs:
                    nkt = 4 * cp + 4
                    qcol = 2 * cp * 128
                    for hh in range(G):
                        h = h0 + hh
                        g = h // REP
                        rr_ap = rrows[0:4, :].rearrange(
                            "p (j g q) -> p j g q", g=G, q=128)[
                            :, 2 * cp:2 * cp + 2, hh, :]
                        prev_pt = None
                        prev_zt = None
                        for t in range(nkt):
                            pb = t % 2
                            tm1 = b.t('pe', lambda e, t=t, hh=hh, g=g, pb=pb, qcol=qcol: e.matmul(
                                out=ps_st[pb][:],
                                lhsT=krT[:, g * S + t * 128: g * S + t * 128 + 128],
                                rhs=qr_hg[:, hh * 1024 + qcol: hh * 1024 + qcol + 256],
                                start=True, stop=False),
                                waits=b.now('ac', 've'), inc='pe')
                            tm2 = b.t('pe', lambda e, pb=pb, rr_ap=rr_ap: e.matmul(
                                out=ps_st[pb][:], lhsT=ones4r[:], rhs=rr_ap,
                                start=False, stop=True), inc='pe')
                            tev = b.t('ac', lambda e, pb=pb: e.activation(
                                out=ebuf[:], in_=ps_st[pb][:], func=AF.Exp,
                                scale=ISQ),
                                waits=[('pe', tm2)] +
                                ([prev_pt] if prev_pt else []), inc='ac')
                            li = t - 4 * cp
                            war = [prev_zt] if prev_zt else []
                            if li >= 0:
                                tpt = b.t('ve', lambda e: e.scalar_tensor_tensor(
                                    out=ptbuf[:], in0=ebuf[:], scalar=1.0,
                                    in1=ebuf[:], op0=OP.is_ge, op1=OP.mult),
                                    waits=[('ac', tev)] + war, inc='ve')
                                tpt = b.t('po', lambda e, cp=cp, li=li: e.tensor_tensor(
                                    out=ptbuf[:], in0=ptbuf[:],
                                    in1=ptmasks[:, (cp * 4 + li) * 256:
                                                (cp * 4 + li) * 256 + 256],
                                    op=OP.mult),
                                    waits=[('ve', tpt), ('gd', a3)], inc='po')
                                ptw = ('po', tpt)
                            else:
                                tpt = b.t('ve', lambda e: e.scalar_tensor_tensor(
                                    out=ptbuf[:], in0=ebuf[:], scalar=1.0,
                                    in1=ebuf[:], op0=OP.is_ge, op1=OP.mult),
                                    waits=[('ac', tev)] + war, inc='ve')
                                ptw = ('ve', tpt)
                            prev_pt = ptw
                            tav = b.t('pe', lambda e, t=t, g=g, nkt=nkt: e.matmul(
                                out=ps_av[:],
                                lhsT=vkm[:, (g * 16 + t) * 128:(g * 16 + t) * 128 + 128],
                                rhs=ptbuf[:], start=(t == 0), stop=(t == nkt - 1)),
                                waits=[ptw], inc='pe')
                            tz = b.t('pe', lambda e, t=t, nkt=nkt: e.matmul(
                                out=ps_z[:], lhsT=ones128[:], rhs=ptbuf[:],
                                start=(t == 0), stop=(t == nkt - 1)),
                                inc='pe')
                            prev_zt = ('pe', tz)
                        tz1 = b.t('ve', lambda e: e.tensor_copy(out=zrow[:],
                                                                in_=ps_z[:]),
                                  waits=[('pe', tz)], inc='ve')
                        tz2 = b.t('ve', lambda e: e.reciprocal(out=zinv[:],
                                                               in_=zrow[:]),
                                  inc='ve')
                        tz3 = b.t('sy', lambda e: e.dma_start(
                            out=zdram, in_=zinv[:]),
                            waits=[('ve', tz2)], inc='st', amt=16)
                        tz4 = b.t('sy', lambda e: e.dma_start(
                            out=zbf[:],
                            in_=zdram.rearrange("p (o n) -> p o n", o=1)
                            .to_broadcast([1, 128, 256])),
                            waits=[('st', tz3), ('ve', b.ctr['ve'])],
                            inc='ld', amt=16)
                        tae = b.t('ve', lambda e: e.tensor_tensor(
                            out=attb[:], in0=ps_av[:], in1=zbf[:], op=OP.mult),
                            waits=[('pe', tav), ('ld', tz4), ('st', b.ctr['st'])],
                            inc='ve')
                        b.t('sy', lambda e, h=h, qcol=qcol: e.dma_start(
                            out=attn_hbm[h, :, qcol:qcol + 256], in_=attb[:]),
                            waits=[('ve', tae)], inc='st', amt=16)
    att_st = b.ctr['st']
    p4.close()
    pA.close()

    p5 = contextlib.ExitStack()
    E5 = lambda n, s, d: p5.enter_context(nc.sbuf_tensor(n, s, d))
    biasb = E5("biasb", [1, D], BF16)
    w128 = [E5(f"w128_{i}", [128, 1], F32) for i in range(2)]
    cs16 = E5("cs16", [16, 1], F32)
    ss16 = E5("ss16", [16, 1], F32)
    w16 = E5("w16", [16, 1], F32)
    a4 = b.t('po', lambda e: e.dma_start(out=biasb[:], in_=fbias),
             waits=b.now('pe', 've'), inc='gd', amt=16)
    wl1 = b.dma('sy', cs16[:], coeffs)
    wl1 = b.dma('sy', ss16[:], scales)
    ww = b.t('ve', lambda e: e.tensor_tensor(out=w16[:], in0=cs16[:],
                                             in1=ss16[:], op=OP.mult),
             waits=[('ld', wl1)], inc='ve')
    ww2 = b.t('sy', lambda e: e.dma_start(out=wdram, in_=w16[:]),
              waits=[('ve', ww)], inc='st', amt=16)
    for eg in range(2):
        b.t('sy', lambda e, eg=eg: e.dma_start(
            out=w128[eg][:],
            in_=wdram[eg * 8:eg * 8 + 8].to_broadcast([8, 16, 1])),
            waits=[('st', ww2)], inc='ld', amt=16)
    xnoT2 = E5("xnoT2", [128, 32 * 1024], BF16)
    wblk = [E5(f"wblk{i}", [128, 32 * 512], BF16) for i in range(2)]
    bfb = E5("bfb", [128, 2 * D], BF16)
    xab = E5("xab", [128, 2 * 1024], BF16)
    atile = [E5(f"atile{i}", [128, 32 * 128], BF16) for i in range(2)]
    xot = [E5(f"xot{i}", [128, 512], F32) for i in range(2)]
    osb = [E5(f"osb{i}", [128, 512], F32) for i in range(2)]
    ps5 = [p5.enter_context(nc.psum_tensor(f"ps5_{i}", [128, 512], F32))
           for i in range(8)]

    xrl = b.dma('sy', xnoT2[:], xno_hbm, waits=b.now('pe', 've'))
    bfl = b.t('po', lambda e: e.dma_start(
        out=bfb[:].rearrange("p (g n) -> p g n", g=2),
        in_=bf_st.rearrange("(g p) n -> p g n", p=128)), inc='gd', amt=16)
    asl = b.t('po', lambda e: e.dma_start(
        out=wblk[0][:, 0:32 * 256].rearrange("p (c n) -> p c n", n=256),
        in_=chunked_wT(a_st)), inc='gd', amt=16)

    for eg in range(2):
        for tb in range(2):
            for c in range(32):
                tk = b.t('pe', lambda e, eg=eg, tb=tb, c=c: e.matmul(
                    out=ps5[0][:],
                    lhsT=wblk[0][:, c * 256 + eg * 128: c * 256 + eg * 128 + 128],
                    rhs=xnoT2[:, c * 1024 + tb * 512: c * 1024 + tb * 512 + 512],
                    start=(c == 0), stop=(c == 31)),
                    waits=([('gd', asl), ('ld', xrl), ('ld', b.ctr['ld'])]
                           + b.now('ve'))
                    if c == 0 else [], inc='pe' if c == 31 else None)
            b.t('ve', lambda e, eg=eg, tb=tb: e.tensor_scalar(
                out=xab[:, eg * 1024 + tb * 512: eg * 1024 + tb * 512 + 512],
                in0=ps5[0][:], scalar1=w128[eg][:], scalar2=None,
                op0=OP.mult), waits=[('pe', tk)], inc='ve')
    xa_ve = b.ctr['ve']

    for ob in range(8):
        wb = ob % 2
        osl = slice(ob * 512, ob * 512 + 512)
        wol = b.t('po', lambda e, wb=wb, osl=osl: e.dma_start(
            out=wblk[wb][:].rearrange("p (c n) -> p c n", n=512),
            in_=chunked_wT(woT)[:, :, osl]),
            waits=b.now('pe') + [('ve', xa_ve)], inc='gd', amt=16)
        for tt in range(8):
            ab = tt % 2
            tal = b.t('sy', lambda e, tt=tt, ab=ab: e.dma_start(
                out=atile[ab][:].rearrange("p (h n) -> p h n", n=128),
                in_=attn_hbm.rearrange("h p n -> p h n")[:, :, tt * 128:tt * 128 + 128]),
                waits=[('st', att_st)] + b.now('pe'), inc='la', amt=16)
            for hh in range(H):
                tk = b.t('pe', lambda e, tt=tt, hh=hh, wb=wb, ab=ab: e.matmul(
                    out=ps5[tt][:], lhsT=atile[ab][:, hh * 128:(hh + 1) * 128],
                    rhs=wblk[wb][:, hh * 512:(hh + 1) * 512],
                    start=(hh == 0), stop=False),
                    waits=[('la', tal), ('gd', wol)] + b.now('ve')
                    if hh == 0 else [], inc=None)
            b.t('pe', lambda e, tt=tt, wb=wb, ob=ob: e.matmul(
                out=ps5[tt][:], lhsT=ones1r[:],
                rhs=biasb[:, ob * 512:ob * 512 + 512], start=False, stop=False),
                waits=[('gd', a4)], inc='pe')
        wfl = b.t('po', lambda e, wb=wb, osl=osl: e.dma_start(
            out=wblk[1 - wb][:].rearrange("p (c n) -> p c n", n=512),
            in_=chunked_wT(wT)[:, :, osl]), waits=b.now('pe'), inc='gd', amt=16)
        for tt in range(8):
            xol = b.dma('sy', xot[tt % 2][:],
                        x_own[tt * 128:tt * 128 + 128, osl],
                        waits=b.now('ve'), inc='lx')
            for c in range(32):
                tk = b.t('pe', lambda e, tt=tt, c=c, wb=wb: e.matmul(
                    out=ps5[tt][:], lhsT=xnoT2[:, c * 1024 + tt * 128:
                                               c * 1024 + tt * 128 + 128],
                    rhs=wblk[1 - wb][:, c * 512:(c + 1) * 512],
                    start=False, stop=False),
                    waits=[('gd', wfl)] if c == 0 else [], inc=None)
            for eg in range(2):
                tk = b.t('pe', lambda e, tt=tt, eg=eg, ob=ob: e.matmul(
                    out=ps5[tt][:],
                    lhsT=xab[:, eg * 1024 + tt * 128: eg * 1024 + tt * 128 + 128],
                    rhs=bfb[:, eg * D + ob * 512: eg * D + ob * 512 + 512],
                    start=False, stop=(eg == 1)),
                    waits=[('gd', bfl), ('ve', xa_ve)] if eg == 0 else [],
                    inc='pe' if eg == 1 else None)
            tev = b.t('ve', lambda e, tt=tt: e.tensor_tensor(
                out=osb[tt % 2][:], in0=ps5[tt][:], in1=xot[tt % 2][:],
                op=OP.add), waits=[('pe', tk), ('lx', xol),
                                   ('st', b.ctr['st'])], inc='ve')
            b.t('sy', lambda e, tt=tt, osl=osl: e.dma_start(
                out=out_own[tt * 128:tt * 128 + 128, osl],
                in_=osb[tt % 2][:]), waits=[('ve', tev)], inc='st', amt=16)

    final_st = b.ctr['st']
    b.t('sy', lambda e: None, waits=[('st', final_st)])

    with nc.Block() as block:
        b.emit(block)
    p5.close()
    b.es.close()
    return nc


_PROG = None


def _host_aux(p):
    tiles = [2 * j + p for j in range(NSLOT)]
    qtrue = np.concatenate([np.arange(t * 128, t * 128 + 128) for t in tiles])
    selc = np.zeros((128, NSLOT * 8), np.float32)
    for j, t in enumerate(tiles):
        q = np.arange(t * 128, t * 128 + 128)
        nq = (q + 1).astype(np.float32)
        tgt = np.minimum(nq, TGT)
        col = j * 8
        selc[:, col + 0] = tgt
        selc[:, col + 1] = tgt - 4.0
        selc[:, col + 2] = tgt - 32.0
        selc[:, col + 3] = tgt - 64.0
        selc[:, col + 4] = tgt + 64.0
        selc[:, col + 5] = (tgt >= nq).astype(np.float32)
        selc[:, col + 6] = float(LS[j])
        selc[:, col + 7] = nq
    sbm = np.zeros((128, NSLOT * 256), np.float32)
    for j, t in enumerate(tiles):
        kbase = (KT[j] - 2) * 128
        for r in range(128):
            q = t * 128 + r
            ks = np.arange(kbase, kbase + 256)
            sbm[r, j * 256:(j + 1) * 256] = np.where(ks <= q, 0.0, MASKVAL)
    ptm = np.zeros((128, 16 * 256), np.float32)
    for cp in range(4):
        for li in range(4):
            tk = 4 * cp + li
            for c in range(256):
                sl = 2 * cp + c // 128
                q = tiles[sl] * 128 + (c % 128)
                kg = tk * 128 + np.arange(128)
                ptm[:, (cp * 4 + li) * 256 + c] = (kg <= q).astype(np.float32)
    iota = np.tile(np.arange(8, dtype=np.float32), (128, 1))
    return qtrue, selc, sbm, ptm, iota


def _rope_n(tvals):
    inv = (10000.0 ** (-np.arange(64) / 64.0))
    th = np.outer(inv, tvals)
    return np.rint(th / (2 * np.pi)).astype(np.float32)


def kernel(**inputs):
    global _PROG
    x = np.ascontiguousarray(np.asarray(inputs["L0_attn_hidden_states"],
                                        np.float32))
    Wq = np.asarray(inputs["L0_attn_Wq"], np.float32)
    Wk = np.asarray(inputs["L0_attn_Wk"], np.float32)
    Wv = np.asarray(inputs["L0_attn_Wv"], np.float32)
    Wo = np.asarray(inputs["L0_attn_Wo"], np.float32)
    W = np.asarray(inputs["L0_ffn_W"], np.float32)
    fb = np.asarray(inputs["L0_ffn_bias"], np.float32)
    A = np.asarray(inputs["L0_ffn_A"], np.float32)
    Bf = np.asarray(inputs["L0_ffn_Bf"], np.float32)
    co = np.asarray(inputs["L0_ffn_coeffs"], np.float32)
    sc = np.asarray(inputs["L0_ffn_scales"], np.float32)

    wqT = np.ascontiguousarray(Wq.T)
    wkT = np.ascontiguousarray(Wk.T)
    wvT = np.ascontiguousarray(Wv.T)
    woT = np.ascontiguousarray(Wo.T)
    wT = np.ascontiguousarray(W.T)
    a_st = np.ascontiguousarray(A.transpose(2, 0, 1).reshape(D, 256))
    bf_st = np.ascontiguousarray(Bf.transpose(0, 2, 1).reshape(256, D))

    tf = np.arange(S, dtype=np.float32)[None, :]
    nf = _rope_n(np.arange(S))
    ivc = np.arange(64, dtype=np.float32)[:, None]

    if _PROG is None:
        _PROG = build_program()

    in_maps = []
    for c in range(8):
        bb, p = c // 2, c % 2
        qtrue, selc, sbm, ptm, iota = _host_aux(p)
        xo = np.ascontiguousarray(x[bb, qtrue, :])
        tow = qtrue.astype(np.float32)[None, :]
        im = dict(
            x_full=np.ascontiguousarray(x[bb]), x_own=xo,
            wqT=wqT, wkT=wkT, wvT=wvT, woT=woT, wT=wT,
            a_st=a_st, bf_st=bf_st, fbias=fb[None, :],
            coeffs=co[:, None], scales=sc[:, None],
            t_full=tf, t_own=tow, n_full=nf, n_own=_rope_n(qtrue),
            ivec=ivc, selc=selc, sbmask=sbm, ptmask=ptm, iota8=iota,
        )
        in_maps.append(im)

    res = run_bass_kernel_spmd(_PROG, in_maps, core_ids=list(range(8)))
    out = np.zeros((B, S, D), np.float32)
    for c in range(8):
        bb, p = c // 2, c % 2
        qtrue, *_ = _host_aux(p)
        out[bb, qtrue, :] = res.results[c]["out_own"]
    return out
